# version 02
# baseline (speedup 1.0000x reference)
from contextlib import ExitStack
import numpy as np
import ml_dtypes
import concourse.bass as bass
import concourse.mybir as mybir
from concourse.bass_utils import run_bass_kernel_spmd

F32 = mybir.dt.float32
BF16 = mybir.dt.bfloat16
AF = mybir.ActivationFunctionType
ALU = mybir.AluOpType
AXX = mybir.AxisListType.X
BF = ml_dtypes.bfloat16

B, T, D = 2, 8192, 1024
NCORES = 8
TPC = T // 4
DC = D // 128
LN_EPS = 1e-5


class S:

    def __init__(self, nc, block, tk):
        self.nc, self.block, self.tk = nc, block, tk
        self.target = 0

    def step(self, eng_name):
        auto_drain = eng_name in ("vector", "scalar", "gpsimd")

        def deco(fn):
            wait = self.target

            def run(eng):
                eng.wait_ge(self.tk, wait)
                last = fn(_DrainProxy(eng) if auto_drain else eng)
                assert last is not None, "step fn must return its last instruction"
                last.then_inc(self.tk, 1)

            getattr(self.block, eng_name)(run)
            self.target = wait + 1
        return deco

    def dma(self, eng_name):
        def deco(fn):
            wait = self.target
            cell = {}

            def run(eng):
                eng.wait_ge(self.tk, wait)
                insts = fn(eng)
                for i in insts:
                    i.then_inc(self.tk, 16)
                cell["n"] = len(insts)

            getattr(self.block, eng_name)(run)
            self.target = wait + 16 * cell["n"]
        return deco

    def fence(self, eng_name="sync"):
        wait = self.target

        def run(eng):
            eng.wait_ge(self.tk, wait)

        getattr(self.block, eng_name)(run)


class _DrainProxy:

    def __init__(self, eng):
        object.__setattr__(self, "_eng", eng)
        object.__setattr__(self, "_n", 0)

    def __getattr__(self, name):
        eng = object.__getattribute__(self, "_eng")
        attr = getattr(eng, name)
        if name in ("drain", "wait_ge", "then_inc") or not callable(attr):
            return attr

        def wrapped(*a, **k):
            if object.__getattribute__(self, "_n") > 0:
                eng.drain()
            object.__setattr__(self, "_n", object.__getattribute__(self, "_n") + 1)
            return attr(*a, **k)
        return wrapped


def bcast(ap_main, ap_small):
    return bass.broadcast_tensor_aps(ap_main, ap_small)


class Alloc:

    def __init__(self, nc, st):
        self.nc, self.st = nc, st

    def sb(self, name, shape, dtype=F32):
        return self.st.enter_context(self.nc.sbuf_tensor(name, shape, dtype))[:, :]

    def ps(self, name, shape, dtype=F32):
        return self.st.enter_context(self.nc.psum_tensor(name, shape, dtype))[:, :]

    def sem(self, name):
        return self.st.enter_context(self.nc.semaphore(name))

    def block(self):
        return self.st.enter_context(self.nc.Block())


def to_fm(x):
    return np.ascontiguousarray(np.swapaxes(np.asarray(x, dtype=np.float32), -1, -2))


def vec_tile(v):
    v = np.asarray(v, dtype=np.float32)
    return np.ascontiguousarray(v.reshape(-1, 128).T)


def const_inputs():
    return {
        "ones_k": np.ones((128, 1), dtype=BF),
        "ones_b": np.ones((1, 128), dtype=BF),
    }


def emit_weight_load(s, wf32, wb16, hbm_ap, nchunk, mwid):
    for c in range(nchunk):
        @s.dma("sync")
        def _(eng, c=c):
            return [eng.dma_start(out=wf32[:, 0:mwid], in_=hbm_ap[c * 128:(c + 1) * 128, :])]

        @s.step("vector")
        def _(eng, c=c):
            return eng.tensor_copy(wb16[:, c * mwid:(c + 1) * mwid], wf32[:, 0:mwid])


def emit_ln_fm(s, nc, x3, ln3, sq3, srow, srow_bf, ps_stA, ps_stB, ps_a, ps_b, tmp3,
               ones_k, ones_b, nt, nchunk=DC, dim=D, eps=LN_EPS):
    ntmax = srow.shape[1] // 4
    MEAN = srow[0:1, 0:nt]
    VAR = srow[0:1, ntmax:ntmax + nt]
    RSTD = srow[0:1, 2 * ntmax:2 * ntmax + nt]
    BROW = srow[0:1, 3 * ntmax:3 * ntmax + nt]
    RSTD_BF = srow_bf[0:1, 0:nt]
    BROW_BF = srow_bf[0:1, ntmax:ntmax + nt]

    @s.step("vector")
    def _(eng):
        return eng.tensor_copy(ln3, x3)

    @s.step("tensor")
    def _(eng):
        for c in range(nchunk):
            i = eng.matmul(ps_stA[0:1, 0:nt], ones_k, ln3[:, c, :],
                           start=(c == 0), stop=(c == nchunk - 1))
        return i

    @s.step("scalar")
    def _(eng):
        return eng.activation(sq3, ln3, AF.Square)

    @s.step("tensor")
    def _(eng):
        for c in range(nchunk):
            i = eng.matmul(ps_stB[0:1, 0:nt], ones_k, sq3[:, c, :],
                           start=(c == 0), stop=(c == nchunk - 1))
        return i

    @s.step("vector")
    def _(eng):
        eng.tensor_single_scalar(MEAN, ps_stA[0:1, 0:nt], 1.0 / dim, ALU.mult)
        eng.tensor_single_scalar(VAR, ps_stB[0:1, 0:nt], 1.0 / dim, ALU.mult)
        eng.drain()
        eng.tensor_tensor(RSTD, MEAN, MEAN, ALU.mult)
        eng.drain()
        eng.tensor_tensor(VAR, VAR, RSTD, ALU.subtract)
        eng.drain()
        return eng.tensor_single_scalar(VAR, VAR, eps, ALU.add)

    @s.step("scalar")
    def _(eng):
        return eng.activation(VAR, VAR, AF.Sqrt)

    @s.step("vector")
    def _(eng):
        eng.reciprocal(RSTD, VAR)
        eng.drain()
        eng.tensor_tensor(BROW, MEAN, RSTD, ALU.mult)
        eng.drain()
        eng.tensor_single_scalar(BROW, BROW, -1.0, ALU.mult)
        eng.drain()
        eng.tensor_copy(RSTD_BF, RSTD)
        return eng.tensor_copy(BROW_BF, BROW)

    @s.step("tensor")
    def _(eng):
        eng.matmul(ps_a[:, 0:nt], ones_b, RSTD_BF, start=True, stop=True)
        return eng.matmul(ps_b[:, 0:nt], ones_b, BROW_BF, start=True, stop=True)

    @s.step("vector")
    def _(eng):
        am, asm = bcast(x3, ps_a[:, 0:nt].rearrange("p (o t) -> p o t", o=1))
        eng.tensor_tensor(tmp3, am, asm, ALU.mult)
        bm, bsm = bcast(tmp3, ps_b[:, 0:nt].rearrange("p (o t) -> p o t", o=1))
        return eng.tensor_tensor(ln3, bm, bsm, ALU.add)


def build_ffn_half(ntok=TPC, nt=512):
    nsub = ntok // nt
    nc = bass.Bass("TRN2", target_bir_lowering=False, debug=False)
    xs = nc.dram_tensor("xs", [D, ntok + 1], F32, kind="ExternalInput").ap()
    Wr = nc.dram_tensor("Wr", [D, D], F32, kind="ExternalInput").ap()
    Wk = nc.dram_tensor("Wk", [D, D], F32, kind="ExternalInput").ap()
    Wv = nc.dram_tensor("Wv", [D, D], F32, kind="ExternalInput").ap()
    mur = nc.dram_tensor("mur", [128, DC], F32, kind="ExternalInput").ap()
    muk = nc.dram_tensor("muk", [128, DC], F32, kind="ExternalInput").ap()
    onk = nc.dram_tensor("ones_k", [128, 1], BF16, kind="ExternalInput").ap()
    onb = nc.dram_tensor("ones_b", [1, 128], BF16, kind="ExternalInput").ap()
    xo = nc.dram_tensor("xo", [D, ntok], F32, kind="ExternalOutput").ap()

    with ExitStack() as st:
        al = Alloc(nc, st)
        xs_t = al.sb("xs_t", [128, DC * (ntok + 1)], F32)
        ln_t = al.sb("ln_t", [128, DC * nt], BF16)
        lnp_t = al.sb("lnp_t", [128, DC], BF16)
        sq_t = al.sb("sq_t", [128, DC * nt], BF16)
        dd_t = al.sb("dd_t", [128, DC * nt], F32)
        xr_t = al.sb("xr_t", [128, DC * nt], BF16)
        xk_t = al.sb("xk_t", [128, DC * nt], BF16)
        rs_t = al.sb("rs_t", [128, DC * nt], BF16)
        ka_t = al.sb("ka_t", [128, DC * nt], BF16)
        krelu = al.sb("krelu", [128, nt], F32)
        hout = al.sb("hout", [128, nt], F32)
        wstage = al.sb("wstage", [128, D], F32)
        wr_b = al.sb("wr_b", [128, DC * D], BF16)
        wk_b = al.sb("wk_b", [128, DC * D], BF16)
        wv_b = al.sb("wv_b", [128, DC * D], BF16)
        mur_t = al.sb("mur_t", [128, DC], F32)
        muk_t = al.sb("muk_t", [128, DC], F32)
        onk_t = al.sb("onk_t", [128, 1], BF16)
        onb_t = al.sb("onb_t", [1, 128], BF16)
        srow = al.sb("srow", [1, 4 * nt], F32)
        srow_bf = al.sb("srow_bf", [1, 2 * nt], BF16)
        ps_mm = al.ps("ps_mm", [128, nt], F32)
        ps_stA = al.ps("ps_stA", [1, nt], F32)
        ps_stB = al.ps("ps_stB", [1, nt], F32)
        ps_a = al.ps("ps_a", [128, nt], F32)
        ps_b = al.ps("ps_b", [128, nt], F32)
        tk = al.sem("tk")
        block = al.block()

        s = S(nc, block, tk)

        @s.dma("sync")
        def _(eng):
            return [
                eng.dma_start(out=xs_t.rearrange("p (c t) -> p c t", c=DC),
                              in_=xs.rearrange("(c p) t -> p c t", p=128)),
                eng.dma_start(out=mur_t, in_=mur[:, :]),
                eng.dma_start(out=muk_t, in_=muk[:, :]),
                eng.dma_start(out=onk_t, in_=onk[:, :]),
                eng.dma_start(out=onb_t, in_=onb[:, :]),
            ]

        emit_weight_load(s, wstage, wr_b, Wr, DC, D)
        emit_weight_load(s, wstage, wk_b, Wk, DC, D)
        emit_weight_load(s, wstage, wv_b, Wv, DC, D)

        xs3 = xs_t.rearrange("p (c t) -> p c t", c=DC)
        ln3 = ln_t.rearrange("p (c t) -> p c t", c=DC)
        lnp3 = lnp_t.rearrange("p (c o) -> p c o", o=1)
        sq3 = sq_t.rearrange("p (c t) -> p c t", c=DC)
        dd3 = dd_t.rearrange("p (c t) -> p c t", c=DC)
        xr3 = xr_t.rearrange("p (c t) -> p c t", c=DC)
        xk3 = xk_t.rearrange("p (c t) -> p c t", c=DC)
        ka3 = ka_t.rearrange("p (c t) -> p c t", c=DC)
        onk_a, onb_a = onk_t, onb_t

        emit_ln_fm(s, nc, xs3[:, :, 0:1], lnp3, sq3[:, :, 0:1], srow, srow_bf,
                   ps_stA, ps_stB, ps_a, ps_b, dd3[:, :, 0:1], onk_a, onb_a, 1)

        for j in range(nsub):
            lo = 1 + j * nt
            emit_ln_fm(s, nc, xs3[:, :, lo:lo + nt], ln3, sq3, srow, srow_bf,
                       ps_stA, ps_stB, ps_a, ps_b, dd3, onk_a, onb_a, nt)

            @s.step("vector")
            def _(eng):
                eng.tensor_tensor(dd3[:, :, 1:nt], ln3[:, :, 1:nt], ln3[:, :, 0:nt - 1],
                                  ALU.subtract)
                return eng.tensor_tensor(dd3[:, :, 0:1], ln3[:, :, 0:1], lnp3, ALU.subtract)

            @s.step("vector")
            def _(eng):
                for c in range(DC):
                    eng.scalar_tensor_tensor(xr3[:, c, 1:nt], dd3[:, c, 1:nt], mur_t[:, c:c + 1],
                                             ln3[:, c, 0:nt - 1], ALU.mult, ALU.add)
                    eng.scalar_tensor_tensor(xr3[:, c, 0:1], dd3[:, c, 0:1], mur_t[:, c:c + 1],
                                             lnp3[:, c, :], ALU.mult, ALU.add)
                    eng.scalar_tensor_tensor(xk3[:, c, 1:nt], dd3[:, c, 1:nt], muk_t[:, c:c + 1],
                                             ln3[:, c, 0:nt - 1], ALU.mult, ALU.add)
                    i = eng.scalar_tensor_tensor(xk3[:, c, 0:1], dd3[:, c, 0:1], muk_t[:, c:c + 1],
                                                 lnp3[:, c, :], ALU.mult, ALU.add)
                return eng.tensor_copy(lnp3, ln3[:, :, nt - 1:nt])

            for mc in range(DC):
                @s.step("tensor")
                def _(eng, mc=mc):
                    for kc in range(DC):
                        i = eng.matmul(ps_mm[:, :], wr_b[:, kc * D + mc * 128: kc * D + (mc + 1) * 128],
                                       xr3[:, kc, :], start=(kc == 0), stop=(kc == DC - 1))
                    return i

                @s.step("scalar")
                def _(eng, mc=mc):
                    return eng.activation(rs_t[:, mc * nt:(mc + 1) * nt], ps_mm[:, :], AF.Sigmoid)

            for mc in range(DC):
                @s.step("tensor")
                def _(eng, mc=mc):
                    for kc in range(DC):
                        i = eng.matmul(ps_mm[:, :], wk_b[:, kc * D + mc * 128: kc * D + (mc + 1) * 128],
                                       xk3[:, kc, :], start=(kc == 0), stop=(kc == DC - 1))
                    return i

                @s.step("scalar")
                def _(eng, mc=mc):
                    return eng.activation(krelu[:, :], ps_mm[:, :], AF.Relu)

                @s.step("scalar")
                def _(eng, mc=mc):
                    return eng.activation(ka_t[:, mc * nt:(mc + 1) * nt], krelu[:, :], AF.Square)

            for mc in range(DC):
                @s.step("tensor")
                def _(eng, mc=mc):
                    for kc in range(DC):
                        i = eng.matmul(ps_mm[:, :], wv_b[:, kc * D + mc * 128: kc * D + (mc + 1) * 128],
                                       ka3[:, kc, :], start=(kc == 0), stop=(kc == DC - 1))
                    return i

                @s.step("vector")
                def _(eng, mc=mc, lo=lo):
                    eng.tensor_tensor(hout[:, :], rs_t[:, mc * nt:(mc + 1) * nt], ps_mm[:, :], ALU.mult)
                    return eng.scalar_tensor_tensor(hout[:, :], hout[:, :], 0.5,
                                                    xs3[:, mc, lo:lo + nt], ALU.mult, ALU.add)

                @s.dma("sync")
                def _(eng, mc=mc, j=j):
                    return [eng.dma_start(out=xo[mc * 128:(mc + 1) * 128, j * nt:(j + 1) * nt],
                                          in_=hout[:, :])]

        s.fence()
    return nc


def run_ffn_half(x_fm_by_core, halo_by_core, Wr, Wk, Wv, mu_r, mu_k):
    nc = build_ffn_half()
    cst = const_inputs()
    in_maps = []
    for c in range(NCORES):
        xs = np.concatenate([halo_by_core[c], x_fm_by_core[c]], axis=1)
        in_maps.append({
            "xs": np.ascontiguousarray(xs),
            "Wr": np.asarray(Wr, np.float32), "Wk": np.asarray(Wk, np.float32),
            "Wv": np.asarray(Wv, np.float32),
            "mur": vec_tile(mu_r), "muk": vec_tile(mu_k), **cst,
        })
    res = run_bass_kernel_spmd(nc, in_maps, core_ids=list(range(NCORES)))
    return [np.asarray(r["xo"], np.float32) for r in res.results]


def shard_tokens(x_fm_b):
    xs, halos = [], []
    for c in range(NCORES):
        b, q = divmod(c, 4)
        xs.append(np.ascontiguousarray(x_fm_b[b][:, q * TPC:(q + 1) * TPC]))
        if q == 0:
            halos.append(np.zeros((D, 1), np.float32))
        else:
            halos.append(np.ascontiguousarray(x_fm_b[b][:, q * TPC - 1:q * TPC]))
    return xs, halos


def unshard_tokens(slices):
    return [np.concatenate(slices[b * 4:(b + 1) * 4], axis=1) for b in range(B)]


def kernel(**inputs):
    i = inputs
    x0_fm = [np.ascontiguousarray(np.asarray(i["L0_attn_hidden_states"][b], np.float32).T)
             for b in range(B)]

    xs, halos = shard_tokens(x0_fm)
    wrT = np.ascontiguousarray(np.asarray(i["L0_ffn_W_r"], np.float32).T)
    wkT = np.ascontiguousarray(np.asarray(i["L0_ffn_W_k"], np.float32).T)
    wvT = np.ascontiguousarray(np.asarray(i["L0_ffn_W_v"], np.float32).T)
    x1s = run_ffn_half(xs, halos, wrT, wkT, wvT, i["L0_ffn_mu_r"], i["L0_ffn_mu_k"])
    x1_fm = unshard_tokens(x1s)

    o_fm = run_gla(x1_fm, np.asarray(i["L0_attn_Wq"], np.float32),
                   np.asarray(i["L0_attn_Wf"], np.float32),
                   np.asarray(i["L0_attn_Wi"], np.float32))
    x2s = run_attnout(x1s, o_fm, np.asarray(i["L0_attn_Wo"], np.float32),
                      i["L0_attn_g_weight"])
    x2_fm = unshard_tokens(x2s)

    xs2, halos2 = shard_tokens(x2_fm)
    x3s = run_ffn_half(xs2, halos2, wrT, wkT, wvT, i["L0_ffn_mu_r"], i["L0_ffn_mu_k"])

    mi = moe_inputs(i)
    x4s = run_moe(x3s, mi)
    x4_fm = unshard_tokens(x4s)

    prep = run_l1prep(x4_fm, l1prep_inputs(i))
    dfm = run_delta(prep)
    x5s = run_l1out(x4_fm, prep, dfm, l1out_inputs(i))

    x6s = run_moe(x5s, mi)
    x6_fm = unshard_tokens(x6s)
    out = np.stack([np.ascontiguousarray(x6_fm[b].T) for b in range(B)], axis=0)
    return out.astype(np.float32)


GLA_C = 64
GLA_NT = 2048


def build_gla(ntok=T, nt=GLA_NT):
    ntile = ntok // nt
    nch = nt // GLA_C
    CC = GLA_C
    nc = bass.Bass("TRN2", target_bir_lowering=False, debug=False)
    xf = nc.dram_tensor("xf", [D, ntok], F32, kind="ExternalInput").ap()
    Wq = nc.dram_tensor("Wq", [D, 256], F32, kind="ExternalInput").ap()
    Wf = nc.dram_tensor("Wf", [D, 256], F32, kind="ExternalInput").ap()
    Wi = nc.dram_tensor("Wi", [D, 256], F32, kind="ExternalInput").ap()
    onk = nc.dram_tensor("ones_k", [128, 1], BF16, kind="ExternalInput").ap()
    onb = nc.dram_tensor("ones_b", [1, 128], BF16, kind="ExternalInput").ap()
    msku = nc.dram_tensor("mask_u", [CC, CC], F32, kind="ExternalInput").ap()
    idin = nc.dram_tensor("ident", [128, 128], BF16, kind="ExternalInput").ap()
    of = nc.dram_tensor("of", [256, ntok], BF16, kind="ExternalOutput").ap()

    LNSC = float(np.log(128.0 ** -0.5))

    with ExitStack() as stk:
        al = Alloc(nc, stk)
        x_t = al.sb("x_t", [128, DC * nt], F32)
        ln_t = al.sb("ln_t", [128, DC * nt], BF16)
        apad_t = al.sb("apad_t", [128, 2 * (nt + 1)], F32)
        lntmp_t = apad_t[:, 0:DC * 512]
        eb_t = al.sb("eb_t", [128, 2 * nt], BF16)
        qs_t = al.sb("qs_t", [128, 2 * nt], BF16)
        sq_t = qs_t[:, 0:DC * 512]
        qt_t = al.sb("qt_t", [128, 2 * nt], BF16)
        kg_t = al.sb("kg_t", [128, 2 * nt], BF16)
        kt_t = al.sb("kt_t", [128, 2 * nt], BF16)
        kh_t = al.sb("kh_t", [128, 2 * nt], BF16)
        v_t = al.sb("v_t", [128, (nt // 128) * 256], BF16)
        ob_t = al.sb("ob_t", [128, 2 * nt], BF16)
        d_t = al.sb("d_t", [128, 2 * nch], F32)
        zcol = al.sb("zcol", [128, 1], F32)
        lnsc_t = al.sb("lnsc_t", [128, 1], F32)
        carry = al.sb("carry", [128, 2], F32)
        s_f32 = al.sb("s_f32", [128, 2 * 128], F32)
        s_bf = al.sb("s_bf", [128, 2 * 128], BF16)
        pm_t = al.sb("pm_t", [128, 2 * CC], BF16)
        ktm_t = al.sb("ktm_t", [128, 2 * 128], BF16)
        msk_t = al.sb("msk_t", [CC, CC], F32)
        wq_b = al.sb("wq_b", [128, DC * 256], BF16)
        wf_b = al.sb("wf_b", [128, DC * 256], BF16)
        wi_b = al.sb("wi_b", [128, DC * 256], BF16)
        wstage = al.sb("wstage", [128, 256], F32)
        onk_t = al.sb("onk_t", [128, 1], BF16)
        onb_t = al.sb("onb_t", [1, 128], BF16)
        ident = al.sb("ident_t", [128, 128], BF16)
        srow = al.sb("srow", [1, 4 * 512], F32)
        srow_bf = al.sb("srow_bf", [1, 2 * 512], BF16)
        ps_a = al.ps("ps_a", [128, 512], F32)
        ps_b = al.ps("ps_b", [128, 512], F32)
        ps_stA = ps_a[0:1, :]
        ps_stB = ps_b[0:1, :]
        ps_mm = al.ps("ps_mm", [128, 512], F32)
        ps_p = al.ps("ps_p", [64, 2 * CC], F32)
        ps_o = al.ps("ps_o", [128, 2 * CC], F32)
        ps_u = al.ps("ps_u", [128, 2 * 128], F32)
        ps_tr = al.ps("ps_tr", [64, 2 * 128], BF16)
        tk = al.sem("tk")
        block = al.block()

        f_t = x_t[:, 0:2 * nt]
        g_t = x_t[:, 2 * nt:4 * nt]
        sc1_t = x_t[:, 4 * nt:6 * nt]
        arel_t = x_t[:, 6 * nt:8 * nt]

        s = S(nc, block, tk)

        @s.dma("sync")
        def _(eng):
            return [eng.dma_start(out=onk_t, in_=onk[:, :]),
                    eng.dma_start(out=onb_t, in_=onb[:, :]),
                    eng.dma_start(out=msk_t, in_=msku[:, :]),
                    eng.dma_start(out=ident, in_=idin[:, :])]

        emit_weight_load(s, wstage, wq_b, Wq, DC, 256)
        emit_weight_load(s, wstage, wf_b, Wf, DC, 256)
        emit_weight_load(s, wstage, wi_b, Wi, DC, 256)

        @s.step("vector")
        def _(eng):
            eng.memset(zcol, 0.0)
            eng.memset(lnsc_t, LNSC)
            eng.memset(carry, 0.0)
            eng.memset(s_f32, 0.0)
            return eng.memset(s_bf, 0.0)

        x3 = x_t.rearrange("p (c t) -> p c t", c=DC)
        ln3 = ln_t.rearrange("p (c t) -> p c t", c=DC)
        sq3 = sq_t.rearrange("p (c t) -> p c t", c=DC)
        lntmp3 = lntmp_t.rearrange("p (c t) -> p c t", c=DC)

        for j in range(ntile):
            t0 = j * nt
            @s.dma("sync")
            def _(eng, t0=t0):
                return [eng.dma_start(out=x3, in_=xf.rearrange("(c p) t -> p c t", p=128)[:, :, t0:t0 + nt])]

            for u in range(nt // 512):
                emit_ln_fm(s, nc, x3[:, :, u * 512:(u + 1) * 512],
                           ln3[:, :, u * 512:(u + 1) * 512],
                           sq3, srow, srow_bf, ps_stA, ps_stB, ps_a, ps_b,
                           lntmp3, onk_t, onb_t, 512)

            for hc in range(2):
                for nb in range(nt // 512):
                    @s.step("tensor")
                    def _(eng, hc=hc, nb=nb):
                        for kc in range(DC):
                            i = eng.matmul(ps_mm[:, :],
                                           wf_b[:, kc * 256 + hc * 128: kc * 256 + (hc + 1) * 128],
                                           ln3[:, kc, nb * 512:(nb + 1) * 512],
                                           start=(kc == 0), stop=(kc == DC - 1))
                        return i

                    @s.step("scalar")
                    def _(eng, hc=hc, nb=nb):
                        return eng.copy(f_t[:, hc * nt + nb * 512: hc * nt + (nb + 1) * 512], ps_mm[:, :])

            @s.step("scalar")
            def _(eng):
                return eng.activation(kg_t, f_t, AF.Sigmoid, scale=-1.0)

            @s.step("scalar")
            def _(eng):
                return eng.activation(sc1_t, f_t, AF.Relu, scale=-1.0)

            @s.step("vector")
            def _(eng):
                return eng.scalar_tensor_tensor(g_t, f_t, -1.0, f_t, ALU.mult, ALU.max)

            @s.step("scalar")
            def _(eng):
                return eng.activation(g_t, g_t, AF.Exp, scale=-1.0)

            @s.step("scalar")
            def _(eng):
                return eng.activation(g_t, g_t, AF.Ln, bias=1.0)

            @s.step("vector")
            def _(eng):
                eng.tensor_tensor(g_t, g_t, sc1_t, ALU.add)
                return eng.tensor_single_scalar(g_t, g_t, -1.0, ALU.mult)

            @s.step("vector")
            def _(eng):
                eng.tensor_copy(apad_t[:, 0:1], carry[:, 0:1])
                eng.tensor_copy(apad_t[:, nt + 1:nt + 2], carry[:, 1:2])
                d0, dz0 = bcast(g_t[:, 0:nt], zcol)
                eng.tensor_tensor_scan(apad_t[:, 1:nt + 1], d0, dz0, carry[:, 0:1], ALU.add, ALU.add)
                d1, dz1 = bcast(g_t[:, nt:2 * nt], zcol)
                eng.tensor_tensor_scan(apad_t[:, nt + 2:2 * nt + 2], d1, dz1, carry[:, 1:2], ALU.add, ALU.add)
                eng.drain()
                eng.tensor_copy(carry[:, 0:1], apad_t[:, nt:nt + 1])
                return eng.tensor_copy(carry[:, 1:2], apad_t[:, 2 * nt + 1:2 * nt + 2])

            @s.step("vector")
            def _(eng):
                for h in range(2):
                    off = h * (nt + 1)
                    a_in = apad_t[:, off + 1:off + nt + 1].rearrange("p (c t) -> p c t", t=CC)
                    base = apad_t[:, off:off + nt].rearrange("p (c t) -> p c t", t=CC)[:, :, 0:1]
                    am, bm = bcast(a_in, base)
                    eng.tensor_tensor(arel_t[:, h * nt:(h + 1) * nt].rearrange("p (c t) -> p c t", t=CC),
                                      am, bm, ALU.subtract)
                    ends = a_in[:, :, CC - 1:CC]
                    bm2, am2 = bcast(a_in, ends)
                    i = eng.tensor_tensor(sc1_t[:, h * nt:(h + 1) * nt].rearrange("p (c t) -> p c t", t=CC),
                                          am2, bm2, ALU.subtract)
                return i

            @s.step("scalar")
            def _(eng):
                for h in range(2):
                    i = eng.activation(
                        d_t[:, h * nch:(h + 1) * nch].rearrange("p (c o) -> p c o", o=1),
                        arel_t[:, h * nt:(h + 1) * nt].rearrange("p (c t) -> p c t", t=CC)[:, :, CC - 1:CC],
                        AF.Exp)
                return i

            @s.step("scalar")
            def _(eng):
                return eng.activation(eb_t, arel_t, AF.Exp, scale=-1.0)

            @s.step("vector")
            def _(eng):
                return eng.tensor_tensor(kt_t, kg_t, eb_t, ALU.mult)

            @s.step("scalar")
            def _(eng):
                return eng.activation(eb_t, sc1_t, AF.Exp)

            @s.step("vector")
            def _(eng):
                return eng.tensor_tensor(kh_t, kg_t, eb_t, ALU.mult)

            for hc in range(2):
                for nb in range(nt // 512):
                    @s.step("tensor")
                    def _(eng, hc=hc, nb=nb):
                        for kc in range(DC):
                            i = eng.matmul(ps_mm[:, :],
                                           wq_b[:, kc * 256 + hc * 128: kc * 256 + (hc + 1) * 128],
                                           ln3[:, kc, nb * 512:(nb + 1) * 512],
                                           start=(kc == 0), stop=(kc == DC - 1))
                        return i

                    @s.step("scalar")
                    def _(eng, hc=hc, nb=nb):
                        return eng.activation(qs_t[:, hc * nt + nb * 512: hc * nt + (nb + 1) * 512],
                                              ps_mm[:, :], AF.Silu)

            @s.step("scalar")
            def _(eng):
                return eng.activation(eb_t, arel_t, AF.Exp, bias=lnsc_t)

            @s.step("vector")
            def _(eng):
                return eng.tensor_tensor(qt_t, qs_t, eb_t, ALU.mult)

            for tb in range(nt // 128):
                @s.step("tensor")
                def _(eng, tb=tb):
                    for kc in range(DC):
                        i = eng.matmul(ps_mm[:, 0:256],
                                       ln3[:, kc, tb * 128:(tb + 1) * 128],
                                       wi_b[:, kc * 256:(kc + 1) * 256],
                                       start=(kc == 0), stop=(kc == DC - 1))
                    return i

                @s.step("scalar")
                def _(eng, tb=tb):
                    return eng.copy(v_t[:, tb * 256:(tb + 1) * 256], ps_mm[:, 0:256])

            for cc in range(nch):
                c0 = cc * CC
                tb, poff = (c0 // 128), (c0 % 128)

                @s.step("tensor")
                def _(eng, c0=c0):
                    for h in range(2):
                        i = eng.matmul(ps_p[:, h * CC:(h + 1) * CC],
                                       kt_t[:, h * nt + c0:h * nt + c0 + CC],
                                       qt_t[:, h * nt + c0:h * nt + c0 + CC],
                                       start=True, stop=True)
                    return i

                @s.step("vector")
                def _(eng, poff=poff):
                    for h in range(2):
                        i = eng.tensor_tensor(pm_t[poff:poff + CC, h * CC:(h + 1) * CC],
                                              ps_p[:, h * CC:(h + 1) * CC], msk_t, ALU.mult)
                    return i

                @s.step("tensor")
                def _(eng, c0=c0, tb=tb, poff=poff):
                    for h in range(2):
                        eng.matmul(ps_o[:, h * CC:(h + 1) * CC],
                                   s_bf[:, h * 128:(h + 1) * 128],
                                   qt_t[:, h * nt + c0:h * nt + c0 + CC],
                                   start=True, stop=False)
                        i = eng.matmul(ps_o[:, h * CC:(h + 1) * CC],
                                       v_t[poff:poff + CC, tb * 256 + h * 128: tb * 256 + (h + 1) * 128],
                                       pm_t[poff:poff + CC, h * CC:(h + 1) * CC],
                                       start=False, stop=True)
                    return i

                @s.step("scalar")
                def _(eng, c0=c0):
                    for h in range(2):
                        i = eng.copy(ob_t[:, h * nt + c0:h * nt + c0 + CC],
                                     ps_o[:, h * CC:(h + 1) * CC])
                    return i

                @s.step("tensor")
                def _(eng, c0=c0):
                    for h in range(2):
                        i = eng.transpose(ps_tr[:, h * 128:(h + 1) * 128],
                                          kh_t[:, h * nt + c0:h * nt + c0 + CC], ident)
                    return i

                @s.step("vector")
                def _(eng, poff=poff):
                    return eng.tensor_copy(ktm_t[poff:poff + CC, :], ps_tr[:, :])

                @s.step("tensor")
                def _(eng, tb=tb, poff=poff):
                    for h in range(2):
                        i = eng.matmul(ps_u[:, h * 128:(h + 1) * 128],
                                       ktm_t[poff:poff + CC, h * 128:(h + 1) * 128],
                                       v_t[poff:poff + CC, tb * 256 + h * 128: tb * 256 + (h + 1) * 128],
                                       start=True, stop=True)
                    return i

                @s.step("vector")
                def _(eng, cc=cc):
                    for h in range(2):
                        eng.scalar_tensor_tensor(s_f32[:, h * 128:(h + 1) * 128],
                                                 s_f32[:, h * 128:(h + 1) * 128],
                                                 d_t[:, h * nch + cc:h * nch + cc + 1],
                                                 ps_u[:, h * 128:(h + 1) * 128],
                                                 ALU.mult, ALU.add)
                    eng.drain()
                    return eng.tensor_copy(s_bf, s_f32)

            @s.dma("sync")
            def _(eng, t0=t0):
                return [eng.dma_start(out=of[0:128, t0:t0 + nt], in_=ob_t[:, 0:nt]),
                        eng.dma_start(out=of[128:256, t0:t0 + nt], in_=ob_t[:, nt:2 * nt])]

        s.fence()
    return nc


def run_gla(x1_fm, Wq, Wf, Wi):
    nc = build_gla()
    cst = const_inputs()
    msku = np.triu(np.ones((GLA_C, GLA_C), np.float32))
    idm = np.eye(128, dtype=BF)
    in_maps = []
    for c in range(NCORES):
        b, hp = divmod(c, 4)
        sl = slice(hp * 256, (hp + 1) * 256)
        in_maps.append({
            "xf": x1_fm[b],
            "Wq": np.ascontiguousarray(Wq[:, sl]).astype(np.float32),
            "Wf": np.ascontiguousarray(Wf[:, sl]).astype(np.float32),
            "Wi": np.ascontiguousarray(Wi[:, sl]).astype(np.float32),
            "mask_u": msku, "ident": idm, **cst,
        })
    res = run_bass_kernel_spmd(nc, in_maps, core_ids=list(range(NCORES)))
    o_fm = []
    for b in range(B):
        o_fm.append(np.concatenate([np.asarray(res.results[b * 4 + hp]["of"], np.float32)
                                    for hp in range(4)], axis=0))
    return o_fm


def build_attnout(ntok=TPC, nt=512):
    nsub = ntok // nt
    nc = bass.Bass("TRN2", target_bir_lowering=False, debug=False)
    x1 = nc.dram_tensor("x1", [D, ntok], F32, kind="ExternalInput").ap()
    ob = nc.dram_tensor("ob", [D, ntok], BF16, kind="ExternalInput").ap()
    Wo = nc.dram_tensor("Wo", [D, D], F32, kind="ExternalInput").ap()
    gw = nc.dram_tensor("gw", [128, DC], F32, kind="ExternalInput").ap()
    onk = nc.dram_tensor("ones_k", [128, 1], BF16, kind="ExternalInput").ap()
    onb = nc.dram_tensor("ones_b", [1, 128], BF16, kind="ExternalInput").ap()
    x2 = nc.dram_tensor("x2", [D, ntok], F32, kind="ExternalOutput").ap()

    with ExitStack() as stk:
        al = Alloc(nc, stk)
        x1_t = al.sb("x1_t", [128, DC * ntok], F32)
        o_t = al.sb("o_t", [128, DC * ntok], BF16)
        on_t = al.sb("on_t", [128, DC * nt], BF16)
        sq_t = al.sb("sq_t", [128, DC * nt], BF16)
        wo_b = al.sb("wo_b", [128, DC * D], BF16)
        wstage = al.sb("wstage", [128, D], F32)
        gw_t = al.sb("gw_t", [128, DC], F32)
        onk_t = al.sb("onk_t", [128, 1], BF16)
        onb_t = al.sb("onb_t", [1, 128], BF16)
        srow = al.sb("srow", [1, 4 * nt], F32)
        srow_bf = al.sb("srow_bf", [1, 2 * nt], BF16)
        hout = al.sb("hout", [128, nt], F32)
        ps_a = al.ps("ps_a", [128, nt], F32)
        ps_mm = al.ps("ps_mm", [128, nt], F32)
        tk = al.sem("tk")
        block = al.block()

        s = S(nc, block, tk)

        @s.dma("sync")
        def _(eng):
            return [
                eng.dma_start(out=x1_t.rearrange("p (c t) -> p c t", c=DC),
                              in_=x1.rearrange("(c p) t -> p c t", p=128)),
                eng.dma_start(out=o_t.rearrange("p (c t) -> p c t", c=DC),
                              in_=ob.rearrange("(c p) t -> p c t", p=128)),
                eng.dma_start(out=gw_t, in_=gw[:, :]),
                eng.dma_start(out=onk_t, in_=onk[:, :]),
                eng.dma_start(out=onb_t, in_=onb[:, :]),
            ]

        for c in range(DC):
            @s.dma("sync")
            def _(eng, c=c):
                return [eng.dma_start(out=wstage, in_=Wo[c * 128:(c + 1) * 128, :])]

            @s.step("vector")
            def _(eng, c=c):
                return eng.tensor_scalar_mul(wo_b[:, c * D:(c + 1) * D], wstage, gw_t[:, c:c + 1])

        x13 = x1_t.rearrange("p (c t) -> p c t", c=DC)
        o3 = o_t.rearrange("p (c t) -> p c t", c=DC)
        on3 = on_t.rearrange("p (c t) -> p c t", c=DC)
        sq3 = sq_t.rearrange("p (c t) -> p c t", c=DC)
        RSTD = srow[0:1, 0:nt]
        RSTD_BF = srow_bf[0:1, 0:nt]

        for j in range(nsub):
            lo = j * nt
            osl = o3[:, :, lo:lo + nt]

            @s.step("scalar")
            def _(eng):
                return eng.activation(sq3, osl, AF.Square)

            @s.step("tensor")
            def _(eng):
                for c in range(DC):
                    i = eng.matmul(ps_a[0:1, :], onk_t, sq3[:, c, :],
                                   start=(c == 0), stop=(c == DC - 1))
                return i

            @s.step("vector")
            def _(eng):
                eng.tensor_single_scalar(RSTD, ps_a[0:1, :], 1.0 / D, ALU.mult)
                eng.drain()
                return eng.tensor_single_scalar(RSTD, RSTD, LN_EPS, ALU.add)

            @s.step("scalar")
            def _(eng):
                return eng.activation(RSTD, RSTD, AF.Sqrt)

            @s.step("vector")
            def _(eng):
                eng.reciprocal(RSTD, RSTD)
                eng.drain()
                return eng.tensor_copy(RSTD_BF, RSTD)

            @s.step("tensor")
            def _(eng):
                return eng.matmul(ps_a[:, :], onb_t, RSTD_BF, start=True, stop=True)

            @s.step("vector")
            def _(eng):
                am, bm = bcast(osl, ps_a[:, :].rearrange("p (o t) -> p o t", o=1))
                return eng.tensor_tensor(on3, am, bm, ALU.mult)

            for mc in range(DC):
                @s.step("tensor")
                def _(eng, mc=mc):
                    for kc in range(DC):
                        i = eng.matmul(ps_mm[:, :], wo_b[:, kc * D + mc * 128: kc * D + (mc + 1) * 128],
                                       on3[:, kc, :], start=(kc == 0), stop=(kc == DC - 1))
                    return i

                @s.step("vector")
                def _(eng, mc=mc, lo=lo):
                    return eng.tensor_tensor(hout, x13[:, mc, lo:lo + nt], ps_mm[:, :], ALU.add)

                @s.dma("sync")
                def _(eng, mc=mc, j=j):
                    return [eng.dma_start(out=x2[mc * 128:(mc + 1) * 128, j * nt:(j + 1) * nt],
                                          in_=hout)]

        s.fence()
    return nc


def run_attnout(x1_by_core, o_fm, Wo, g_weight):
    nc = build_attnout()
    cst = const_inputs()
    in_maps = []
    for c in range(NCORES):
        b, q = divmod(c, 4)
        in_maps.append({
            "x1": x1_by_core[c],
            "ob": np.ascontiguousarray(o_fm[b][:, q * TPC:(q + 1) * TPC]).astype(BF),
            "Wo": np.asarray(Wo, np.float32),
            "gw": vec_tile(g_weight), **cst,
        })
    res = run_bass_kernel_spmd(nc, in_maps, core_ids=list(range(NCORES)))
    return [np.asarray(r["x2"], np.float32) for r in res.results]


def build_moe(ntok=TPC, nt=512, debug=False):
    nsub = ntok // nt
    nc = bass.Bass("TRN2", target_bir_lowering=False, debug=False)
    xs = nc.dram_tensor("xs", [D, ntok], F32, kind="ExternalInput").ap()
    bwT = nc.dram_tensor("bwT", [D, D], F32, kind="ExternalInput").ap()
    bb = nc.dram_tensor("bb", [128, DC], F32, kind="ExternalInput").ap()
    Aall = nc.dram_tensor("Aall", [D, 128], F32, kind="ExternalInput").ap()
    Ball = nc.dram_tensor("Ball", [128, D], F32, kind="ExternalInput").ap()
    rwT = nc.dram_tensor("rwT", [D, 8], F32, kind="ExternalInput").ap()
    rb = nc.dram_tensor("rb", [8, 1], F32, kind="ExternalInput").ap()
    E2 = nc.dram_tensor("E2", [8, 128], BF16, kind="ExternalInput").ap()
    id8 = nc.dram_tensor("id8", [8, 8], BF16, kind="ExternalInput").ap()
    id128 = nc.dram_tensor("id128", [128, 128], BF16, kind="ExternalInput").ap()
    onk = nc.dram_tensor("ones_k", [128, 1], BF16, kind="ExternalInput").ap()
    onb = nc.dram_tensor("ones_b", [1, 128], BF16, kind="ExternalInput").ap()
    xo = nc.dram_tensor("xo", [D, ntok], F32, kind="ExternalOutput").ap()
    if debug:
        dbg_lg = nc.dram_tensor("dbg_lg", [8, nt], F32, kind="ExternalOutput").ap()
        dbg_wt = nc.dram_tensor("dbg_wt", [128, 4 * 8], F32, kind="ExternalOutput").ap()
        dbg_st = nc.dram_tensor("dbg_st", [128, 32], F32, kind="ExternalOutput").ap()
        dbg_lgtm = nc.dram_tensor("dbg_lgtm", [128, 32], F32, kind="ExternalOutput").ap()
        dbg_ax = nc.dram_tensor("dbg_ax", [128, nt], F32, kind="ExternalOutput").ap()
        dbg_axs = nc.dram_tensor("dbg_axs", [128, nt], F32, kind="ExternalOutput").ap()

    with ExitStack() as stk:
        al = Alloc(nc, stk)
        xs_t = al.sb("xs_t", [128, DC * ntok], F32)
        ln_t = al.sb("ln_t", [128, DC * nt], BF16)
        sq_t = al.sb("sq_t", [128, DC * nt], BF16)
        lntmp_t = al.sb("lntmp_t", [128, DC * nt], F32)
        ax_t = al.sb("ax_t", [128, nt], BF16)
        axs_t = al.sb("axs_t", [128, nt], BF16)
        lg_t = al.sb("lg_t", [8, nt], BF16)
        lgtm_t = al.sb("lgtm_t", [128, 4 * 8], F32)
        wtok_t = al.sb("wtok_t", [128, 4 * 8], BF16)
        wfm_t = al.sb("wfm_t", [8, nt], BF16)
        srow = al.sb("srow", [1, 4 * nt], F32)
        srow_bf = al.sb("srow_bf", [1, 2 * nt], BF16)
        stat_t = al.sb("stat_t", [128, 8 * 4], F32)
        hout = al.sb("hout", [128, nt], F32)
        wb_b = al.sb("wb_b", [128, DC * D], BF16)
        aa_b = al.sb("aa_b", [128, DC * 128], BF16)
        ba_b = al.sb("ba_b", [128, D], BF16)
        rw_b = al.sb("rw_b", [128, DC * 8], BF16)
        wstage = al.sb("wstage", [128, D], F32)
        bb_t = al.sb("bb_t", [128, DC], F32)
        rb_t = al.sb("rb_t", [8, 1], F32)
        e2_t = al.sb("e2_t", [8, 128], BF16)
        id8_t = al.sb("id8_t", [8, 8], BF16)
        id128_t = al.sb("id128_t", [128, 128], BF16)
        onk_t = al.sb("onk_t", [128, 1], BF16)
        onb_t = al.sb("onb_t", [1, 128], BF16)
        ps_a = al.ps("ps_a", [128, nt], F32)
        ps_b = al.ps("ps_b", [128, nt], F32)
        ps_mm = al.ps("ps_mm", [128, nt], F32)
        ps_ax = al.ps("ps_ax", [128, nt], F32)
        ps_lg = al.ps("ps_lg", [8, nt], F32)
        ps_wb = al.ps("ps_wb", [128, nt], F32)
        dbg_lg_t = al.sb("dbg_lg_t", [8, nt], F32)
        dbg_wt_t = al.sb("dbg_wt_t", [128, 4 * 8], F32)
        dbg_ax_t = al.sb("dbg_ax_t", [128, nt], F32)
        dbg_axs_t = al.sb("dbg_axs_t", [128, nt], F32)
        ps_t1 = al.ps("ps_t1", [128, 8], BF16)
        ps_t2 = al.ps("ps_t2", [8, 128], BF16)
        tk = al.sem("tk")
        block = al.block()
        ps_stA = ps_a[0:1, :]
        ps_stB = ps_b[0:1, :]

        s = S(nc, block, tk)

        @s.dma("sync")
        def _(eng):
            return [
                eng.dma_start(out=xs_t.rearrange("p (c t) -> p c t", c=DC),
                              in_=xs.rearrange("(c p) t -> p c t", p=128)),
                eng.dma_start(out=bb_t, in_=bb[:, :]),
                eng.dma_start(out=rb_t, in_=rb[:, :]),
                eng.dma_start(out=e2_t, in_=E2[:, :]),
                eng.dma_start(out=id8_t, in_=id8[:, :]),
                eng.dma_start(out=id128_t, in_=id128[:, :]),
                eng.dma_start(out=onk_t, in_=onk[:, :]),
                eng.dma_start(out=onb_t, in_=onb[:, :]),
            ]

        emit_weight_load(s, wstage, wb_b, bwT, DC, D)
        emit_weight_load(s, wstage, aa_b, Aall, DC, 128)
        emit_weight_load(s, wstage, ba_b, Ball, 1, D)
        emit_weight_load(s, wstage, rw_b, rwT, DC, 8)

        xs3 = xs_t.rearrange("p (c t) -> p c t", c=DC)
        ln3 = ln_t.rearrange("p (c t) -> p c t", c=DC)
        sq3 = sq_t.rearrange("p (c t) -> p c t", c=DC)
        lntmp3 = lntmp_t.rearrange("p (c t) -> p c t", c=DC)

        NB = nt // 128
        M1 = stat_t[:, 0:NB * 1].rearrange("p (b o) -> p b o", o=1)

        for j in range(nsub):
            lo = j * nt
            emit_ln_fm(s, nc, xs3[:, :, lo:lo + nt], ln3, sq3, srow, srow_bf,
                       ps_stA, ps_stB, ps_a, ps_b, lntmp3, onk_t, onb_t, nt)

            @s.step("tensor")
            def _(eng):
                for kc in range(DC):
                    i = eng.matmul(ps_ax[:, :], aa_b[:, kc * 128:(kc + 1) * 128],
                                   ln3[:, kc, :], start=(kc == 0), stop=(kc == DC - 1))
                return i

            @s.step("vector")
            def _(eng):
                return eng.tensor_copy(ax_t, ps_ax[:, :])

            @s.step("tensor")
            def _(eng):
                for kc in range(DC):
                    i = eng.matmul(ps_lg[:, :], rw_b[:, kc * 8:(kc + 1) * 8],
                                   ln3[:, kc, :], start=(kc == 0), stop=(kc == DC - 1))
                return i

            @s.step("vector")
            def _(eng):
                return eng.tensor_scalar_add(lg_t, ps_lg[:, :], rb_t[:, 0:1])

            for nb in range(NB):
                @s.step("tensor")
                def _(eng, nb=nb):
                    return eng.transpose(ps_t1[:, :], lg_t[:, nb * 128:(nb + 1) * 128], id8_t)

                @s.step("vector")
                def _(eng, nb=nb):
                    return eng.tensor_copy(lgtm_t[:, nb * 8:(nb + 1) * 8], ps_t1[:, :])

            for nb in range(NB):
                LG = lgtm_t[:, nb * 8:(nb + 1) * 8]
                m1 = stat_t[:, 0:1]; cnt = stat_t[:, 1:2]; m2 = stat_t[:, 2:3]
                kth = stat_t[:, 3:4]; nm1 = stat_t[:, 4:5]; ssum = stat_t[:, 5:6]
                eqb = stat_t[:, 8:16]; rbuf = stat_t[:, 16:24]; ebuf = stat_t[:, 24:32]

                @s.step("vector")
                def _(eng, LG=LG, m1=m1, cnt=cnt, m2=m2, kth=kth, nm1=nm1,
                      ssum=ssum, eqb=eqb, rbuf=rbuf, ebuf=ebuf, nb=nb):
                    def D():
                        eng.drain()
                    eng.tensor_reduce(m1, LG, AXX, ALU.max); D()
                    am, bm = bcast(LG, m1)
                    eng.tensor_tensor(eqb, am, bm, ALU.subtract); D()
                    eng.tensor_single_scalar(eqb, eqb, 0.0, ALU.is_equal); D()
                    eng.tensor_reduce(cnt, eqb, AXX, ALU.add)
                    eng.scalar_tensor_tensor(rbuf, eqb, -1e9, LG, ALU.mult, ALU.add); D()
                    eng.tensor_reduce(m2, rbuf, AXX, ALU.max); D()
                    eng.tensor_single_scalar(cnt, cnt, 2.0, ALU.is_ge); D()
                    eng.tensor_tensor(kth, m1, m2, ALU.subtract); D()
                    eng.tensor_tensor(kth, kth, cnt, ALU.mult); D()
                    eng.tensor_tensor(kth, kth, m2, ALU.add); D()
                    am2, bm2 = bcast(LG, kth)
                    eng.tensor_tensor(eqb, am2, bm2, ALU.subtract); D()
                    eng.tensor_single_scalar(eqb, eqb, 0.0, ALU.is_ge); D()
                    return eng.tensor_single_scalar(nm1, m1, -1.0, ALU.mult)

                @s.step("scalar")
                def _(eng, LG=LG, nm1=nm1, ebuf=ebuf):
                    return eng.activation(ebuf, LG, AF.Exp, bias=nm1)

                @s.step("vector")
                def _(eng, eqb=eqb, ebuf=ebuf, ssum=ssum, nb=nb):
                    eng.tensor_tensor(ebuf, ebuf, eqb, ALU.mult)
                    eng.drain()
                    eng.tensor_reduce(ssum, ebuf, AXX, ALU.add)
                    eng.drain()
                    eng.reciprocal(ssum, ssum)
                    eng.drain()
                    am, bm = bcast(ebuf, ssum)
                    return eng.tensor_tensor(wtok_t[:, nb * 8:(nb + 1) * 8], am, bm, ALU.mult)

            for nb in range(NB):
                @s.step("tensor")
                def _(eng, nb=nb):
                    return eng.transpose(ps_t2[:, :], wtok_t[:, nb * 8:(nb + 1) * 8], id128_t)

                @s.step("vector")
                def _(eng, nb=nb):
                    return eng.tensor_copy(wfm_t[:, nb * 128:(nb + 1) * 128], ps_t2[:, :])

            @s.step("tensor")
            def _(eng):
                return eng.matmul(ps_wb[:, :], e2_t, wfm_t, start=True, stop=True)

            @s.step("vector")
            def _(eng):
                return eng.tensor_tensor(axs_t, ax_t, ps_wb[:, :], ALU.mult)

            if j == 0:
                @s.step("vector")
                def _(eng):
                    eng.tensor_copy(dbg_lg_t, lg_t)
                    eng.tensor_copy(dbg_wt_t, wtok_t)
                    eng.tensor_copy(dbg_ax_t, ax_t)
                    return eng.tensor_copy(dbg_axs_t, axs_t)

            for mc in range(DC):
                @s.step("tensor")
                def _(eng, mc=mc):
                    for kc in range(DC):
                        eng.matmul(ps_mm[:, :], wb_b[:, kc * D + mc * 128: kc * D + (mc + 1) * 128],
                                   ln3[:, kc, :], start=(kc == 0), stop=False)
                    return eng.matmul(ps_mm[:, :], ba_b[:, mc * 128:(mc + 1) * 128],
                                      axs_t, start=False, stop=True)

                @s.step("vector")
                def _(eng, mc=mc, lo=lo):
                    eng.tensor_scalar_add(hout, ps_mm[:, :], bb_t[:, mc:mc + 1])
                    return eng.scalar_tensor_tensor(hout, hout, 0.5,
                                                    xs3[:, mc, lo:lo + nt], ALU.mult, ALU.add)

                @s.dma("sync")
                def _(eng, mc=mc, j=j):
                    return [eng.dma_start(out=xo[mc * 128:(mc + 1) * 128, j * nt:(j + 1) * nt],
                                          in_=hout)]

        if debug:
            @s.dma("sync")
            def _(eng):
                return [eng.dma_start(out=dbg_st[:, :], in_=stat_t),
                        eng.dma_start(out=dbg_lgtm[:, :], in_=lgtm_t),
                        eng.dma_start(out=dbg_lg[:, :], in_=dbg_lg_t),
                        eng.dma_start(out=dbg_wt[:, :], in_=dbg_wt_t),
                        eng.dma_start(out=dbg_ax[:, :], in_=dbg_ax_t),
                        eng.dma_start(out=dbg_axs[:, :], in_=dbg_axs_t)]

        s.fence()
    return nc


def moe_inputs(i, pref="L1_ffn_"):
    lora_A = np.asarray(i[pref + "lora_A"], np.float32)
    lora_B = np.asarray(i[pref + "lora_B"], np.float32)
    E2 = np.zeros((8, 128), dtype=BF)
    for e in range(8):
        E2[e, e * 16:(e + 1) * 16] = 2.0
    return {
        "bwT": np.ascontiguousarray(np.asarray(i[pref + "base_w"], np.float32).T),
        "bb": vec_tile(i[pref + "base_b"]),
        "Aall": np.ascontiguousarray(lora_A.transpose(1, 0, 2).reshape(D, 128)),
        "Ball": np.ascontiguousarray(lora_B.reshape(128, D)),
        "rwT": np.ascontiguousarray(np.asarray(i[pref + "router_w"], np.float32).T),
        "rb": np.ascontiguousarray(np.asarray(i[pref + "router_b"], np.float32).reshape(8, 1)),
        "E2": E2,
        "id8": np.eye(8, dtype=BF),
        "id128": np.eye(128, dtype=BF),
    }


def run_moe(x_by_core, minputs):
    nc = build_moe()
    cst = const_inputs()
    in_maps = [{"xs": x_by_core[c], **minputs, **cst} for c in range(NCORES)]
    res = run_bass_kernel_spmd(nc, in_maps, core_ids=list(range(NCORES)))
    return [np.asarray(r["xo"], np.float32) for r in res.results]


HLO = 32


def build_l1prep(ntok=TPC, nt=512):
    ntot = ntok + HLO
    nc = bass.Bass("TRN2", target_bir_lowering=False, debug=False)
    xs = nc.dram_tensor("xs", [D, ntot], F32, kind="ExternalInput").ap()
    Wq = nc.dram_tensor("Wq", [D, D], F32, kind="ExternalInput").ap()
    Wk = nc.dram_tensor("Wk", [D, D], F32, kind="ExternalInput").ap()
    Wv = nc.dram_tensor("Wv", [D, D], F32, kind="ExternalInput").ap()
    Wb = nc.dram_tensor("Wb", [D, 4], F32, kind="ExternalInput").ap()
    cq = nc.dram_tensor("cq", [128, DC * 4], F32, kind="ExternalInput").ap()
    ck = nc.dram_tensor("ck", [128, DC * 4], F32, kind="ExternalInput").ap()
    cv = nc.dram_tensor("cv", [128, DC * 4], F32, kind="ExternalInput").ap()
    lw = nc.dram_tensor("lw", [128, DC * 7], F32, kind="ExternalInput").ap()
    mw = nc.dram_tensor("mw", [128, DC * 25], F32, kind="ExternalInput").ap()
    onk = nc.dram_tensor("ones_k", [128, 1], BF16, kind="ExternalInput").ap()
    onb = nc.dram_tensor("ones_b", [1, 128], BF16, kind="ExternalInput").ap()
    qf = nc.dram_tensor("qf", [D, ntok], BF16, kind="ExternalOutput").ap()
    kf = nc.dram_tensor("kf", [D, ntok], BF16, kind="ExternalOutput").ap()
    vf = nc.dram_tensor("vf", [D, ntok], BF16, kind="ExternalOutput").ap()
    beta = nc.dram_tensor("beta", [4, ntok], F32, kind="ExternalOutput").ap()
    loc = nc.dram_tensor("loc", [D, ntok], BF16, kind="ExternalOutput").ap()
    mid = nc.dram_tensor("mid", [D, ntok], BF16, kind="ExternalOutput").ap()

    with ExitStack() as stk:
        al = Alloc(nc, stk)
        xstage = al.sb("xstage", [128, DC * 512], F32)
        ln_t = al.sb("ln_t", [128, DC * ntot], BF16)
        sq_t = al.sb("sq_t", [128, DC * 512], BF16)
        lntmp_t = al.sb("lntmp_t", [128, DC * 512], F32)
        acc0 = al.sb("acc0", [128, ntot], F32)
        acc1 = al.sb("acc1", [128, ntot], F32)
        oout = al.sb("oout", [128, ntok], BF16)
        w_b = al.sb("w_b", [128, DC * D], BF16)
        wb_b = al.sb("wb_b", [128, DC * 4], BF16)
        wstage = al.sb("wstage", [128, D], F32)
        cq_t = al.sb("cq_t", [128, DC * 4], F32)
        ck_t = al.sb("ck_t", [128, DC * 4], F32)
        cv_t = al.sb("cv_t", [128, DC * 4], F32)
        lw_t = al.sb("lw_t", [128, DC * 7], F32)
        mw_t = al.sb("mw_t", [128, DC * 25], F32)
        onk_t = al.sb("onk_t", [128, 1], BF16)
        onb_t = al.sb("onb_t", [1, 128], BF16)
        srow = al.sb("srow", [1, 4 * 512], F32)
        srow_bf = al.sb("srow_bf", [1, 2 * 512], BF16)
        bt_t = al.sb("bt_t", [4, ntok], F32)
        ps_a = al.ps("ps_a", [128, 512], F32)
        ps_b = al.ps("ps_b", [128, 512], F32)
        ps_mm = al.ps("ps_mm", [128, 512], F32)
        ps_bt = al.ps("ps_bt", [4, 512], F32)
        tk = al.sem("tk")
        block = al.block()
        ps_stA = ps_a[0:1, :]
        ps_stB = ps_b[0:1, :]
        s = S(nc, block, tk)
        zbuf = al.sb("zbuf", [128, DC * ntot], BF16)
        vflat = al.sb("vflat", [128, DC * ntot], BF16)

        @s.dma("sync")
        def _(eng):
            return [
                eng.dma_start(out=cq_t, in_=cq[:, :]), eng.dma_start(out=ck_t, in_=ck[:, :]),
                eng.dma_start(out=cv_t, in_=cv[:, :]), eng.dma_start(out=lw_t, in_=lw[:, :]),
                eng.dma_start(out=mw_t, in_=mw[:, :]), eng.dma_start(out=onk_t, in_=onk[:, :]),
                eng.dma_start(out=onb_t, in_=onb[:, :]),
            ]

        xs3 = xstage.rearrange("p (c t) -> p c t", c=DC)
        ln3 = ln_t.rearrange("p (c t) -> p c t", c=DC)
        sq3 = sq_t.rearrange("p (c t) -> p c t", c=DC)
        lntmp3 = lntmp_t.rearrange("p (c t) -> p c t", c=DC)
        zb3 = zbuf.rearrange("p (c t) -> p c t", c=DC)
        vf3 = vflat.rearrange("p (c t) -> p c t", c=DC)

        for u in range((ntot + 511) // 512):
            c0, c1 = u * 512, min((u + 1) * 512, ntot)

            @s.dma("sync")
            def _(eng, c0=c0, c1=c1):
                return [eng.dma_start(out=xs3[:, :, 0:c1 - c0],
                                      in_=xs.rearrange("(c p) t -> p c t", p=128)[:, :, c0:c1])]

            emit_ln_fm(s, nc, xs3[:, :, 0:c1 - c0], ln3[:, :, c0:c1],
                       sq3[:, :, 0:c1 - c0], srow, srow_bf, ps_stA, ps_stB,
                       ps_a, ps_b, lntmp3[:, :, 0:c1 - c0], onk_t, onb_t, c1 - c0)

        emit_weight_load(s, wstage, wb_b, Wb, DC, 4)
        for u in range(ntok // 512):
            @s.step("tensor")
            def _(eng, u=u):
                for kc in range(DC):
                    i = eng.matmul(ps_bt[:, :], wb_b[:, kc * 4:(kc + 1) * 4],
                                   ln3[:, kc, HLO + u * 512:HLO + (u + 1) * 512],
                                   start=(kc == 0), stop=(kc == DC - 1))
                return i

            @s.step("scalar")
            def _(eng, u=u):
                return eng.activation(bt_t[:, u * 512:(u + 1) * 512], ps_bt[:, :], AF.Sigmoid)

        @s.dma("sync")
        def _(eng):
            return [eng.dma_start(out=beta[:, :], in_=bt_t)]

        def proj_conv(W_hbm, cw_t, out_hbm, keep_vflat):
            emit_weight_load(s, wstage, w_b, W_hbm, DC, D)
            for mc in range(DC):
                for u in range((ntot + 511) // 512):
                    c0, c1 = u * 512, min((u + 1) * 512, ntot)

                    @s.step("tensor")
                    def _(eng, mc=mc, c0=c0, c1=c1):
                        for kc in range(DC):
                            i = eng.matmul(ps_mm[:, 0:c1 - c0],
                                           w_b[:, kc * D + mc * 128: kc * D + (mc + 1) * 128],
                                           ln3[:, kc, c0:c1], start=(kc == 0), stop=(kc == DC - 1))
                        return i

                    @s.step("scalar")
                    def _(eng, mc=mc, c0=c0, c1=c1):
                        return eng.copy(zb3[:, mc, c0:c1], ps_mm[:, 0:c1 - c0])

            for mc in range(DC):
                @s.step("vector")
                def _(eng, mc=mc):
                    a0, a1 = acc0[:, 0:ntok], acc1[:, 0:ntok]
                    eng.tensor_scalar_mul(a0, zb3[:, mc, HLO - 3:HLO - 3 + ntok], cw_t[:, mc * 4 + 0:mc * 4 + 1])
                    eng.scalar_tensor_tensor(a1, zb3[:, mc, HLO - 2:HLO - 2 + ntok], cw_t[:, mc * 4 + 1:mc * 4 + 2], a0, ALU.mult, ALU.add)
                    eng.scalar_tensor_tensor(a0, zb3[:, mc, HLO - 1:HLO - 1 + ntok], cw_t[:, mc * 4 + 2:mc * 4 + 3], a1, ALU.mult, ALU.add)
                    return eng.scalar_tensor_tensor(a1, zb3[:, mc, HLO:HLO + ntok], cw_t[:, mc * 4 + 3:mc * 4 + 4], a0, ALU.mult, ALU.add)

                @s.step("scalar")
                def _(eng, mc=mc):
                    return eng.activation(oout, acc1[:, 0:ntok], AF.Silu)

                @s.dma("sync")
                def _(eng, mc=mc):
                    return [eng.dma_start(out=out_hbm[mc * 128:(mc + 1) * 128, :], in_=oout)]

                if keep_vflat:
                    pass

        emit_weight_load(s, wstage, w_b, Wv, DC, D)
        for mc in range(DC):
            for u in range((ntot + 511) // 512):
                c0, c1 = u * 512, min((u + 1) * 512, ntot)

                @s.step("tensor")
                def _(eng, mc=mc, c0=c0, c1=c1):
                    for kc in range(DC):
                        i = eng.matmul(ps_mm[:, 0:c1 - c0],
                                       w_b[:, kc * D + mc * 128: kc * D + (mc + 1) * 128],
                                       ln3[:, kc, c0:c1], start=(kc == 0), stop=(kc == DC - 1))
                    return i

                @s.step("scalar")
                def _(eng, mc=mc, c0=c0, c1=c1):
                    return eng.copy(zb3[:, mc, c0:c1], ps_mm[:, 0:c1 - c0])

        for mc in range(DC):
            @s.step("vector")
            def _(eng, mc=mc):
                nw = ntot - 3
                eng.tensor_scalar_mul(acc0[:, 0:nw], zb3[:, mc, 0:nw], cv_t[:, mc * 4 + 0:mc * 4 + 1])
                eng.scalar_tensor_tensor(acc1[:, 0:nw], zb3[:, mc, 1:1 + nw], cv_t[:, mc * 4 + 1:mc * 4 + 2], acc0[:, 0:nw], ALU.mult, ALU.add)
                eng.scalar_tensor_tensor(acc0[:, 0:nw], zb3[:, mc, 2:2 + nw], cv_t[:, mc * 4 + 2:mc * 4 + 3], acc1[:, 0:nw], ALU.mult, ALU.add)
                return eng.scalar_tensor_tensor(acc1[:, 0:nw], zb3[:, mc, 3:3 + nw], cv_t[:, mc * 4 + 3:mc * 4 + 4], acc0[:, 0:nw], ALU.mult, ALU.add)

            @s.step("scalar")
            def _(eng, mc=mc):
                return eng.activation(vf3[:, mc, 3:ntot], acc1[:, 0:ntot - 3], AF.Silu)

            @s.dma("sync")
            def _(eng, mc=mc):
                return [eng.dma_start(out=vf[mc * 128:(mc + 1) * 128, :], in_=vf3[:, mc, HLO:ntot])]

        for name, w_t, taps, out_hbm in (("loc", lw_t, 7, loc), ("mid", mw_t, 25, mid)):
            for mc in range(DC):
                @s.step("vector")
                def _(eng, mc=mc, w_t=w_t, taps=taps):
                    a, b = acc0[:, 0:ntok], acc1[:, 0:ntok]
                    eng.tensor_scalar_mul(a, vf3[:, mc, HLO - (taps - 1):HLO - (taps - 1) + ntok],
                                          w_t[:, mc * taps + 0:mc * taps + 1])
                    for tp in range(1, taps):
                        sh = HLO - (taps - 1) + tp
                        i = eng.scalar_tensor_tensor(b, vf3[:, mc, sh:sh + ntok],
                                                     w_t[:, mc * taps + tp:mc * taps + tp + 1],
                                                     a, ALU.mult, ALU.add)
                        a, b = b, a
                    return eng.tensor_copy(oout, a)

                @s.dma("sync")
                def _(eng, mc=mc, out_hbm=out_hbm):
                    return [eng.dma_start(out=out_hbm[mc * 128:(mc + 1) * 128, :], in_=oout)]

        proj_conv(Wq, cq_t, qf, False)
        proj_conv(Wk, ck_t, kf, False)

        s.fence()
    return nc


def l1prep_inputs(i):
    def cvt(w, taps):
        w = np.asarray(w, np.float32)
        return np.ascontiguousarray(w.reshape(DC, 128, taps).transpose(1, 0, 2).reshape(128, DC * taps))
    return {
        "Wq": np.asarray(i["L1_attn_Wq"], np.float32),
        "Wk": np.asarray(i["L1_attn_Wk"], np.float32),
        "Wv": np.asarray(i["L1_attn_Wv"], np.float32),
        "Wb": np.asarray(i["L1_attn_Wb"], np.float32),
        "cq": cvt(i["L1_attn_conv_q_w"], 4), "ck": cvt(i["L1_attn_conv_k_w"], 4),
        "cv": cvt(i["L1_attn_conv_v_w"], 4), "lw": cvt(i["L1_attn_local_w"], 7),
        "mw": cvt(i["L1_attn_mid_w"], 25),
    }


def run_l1prep(x4_fm, pinputs):
    nc = build_l1prep()
    cst = const_inputs()
    in_maps = []
    for c in range(NCORES):
        b, q = divmod(c, 4)
        lo = q * TPC - HLO
        if q == 0:
            xs = np.concatenate([np.zeros((D, HLO), np.float32),
                                 x4_fm[b][:, 0:TPC]], axis=1)
        else:
            xs = np.ascontiguousarray(x4_fm[b][:, lo:(q + 1) * TPC])
        in_maps.append({"xs": xs, **pinputs, **cst})
    res = run_bass_kernel_spmd(nc, in_maps, core_ids=list(range(NCORES)))
    out = {}
    for key in ("qf", "kf", "vf", "loc", "mid"):
        out[key] = [np.concatenate([np.asarray(res.results[b * 4 + q][key]) for q in range(4)], axis=1)
                    for b in range(B)]
    out["beta"] = [np.concatenate([np.asarray(res.results[b * 4 + q]["beta"]) for q in range(4)], axis=1)
                   for b in range(B)]
    return out


DLT_C = 64
DLT_NT = 2048


def build_delta(ntok=T, nt=DLT_NT):
    ntile = ntok // nt
    nch = nt // DLT_C
    CC = DLT_C
    nc = bass.Bass("TRN2", target_bir_lowering=False, debug=False)
    qfm = nc.dram_tensor("qfm", [256, ntok], BF16, kind="ExternalInput").ap()
    kfm = nc.dram_tensor("kfm", [256, ntok], BF16, kind="ExternalInput").ap()
    ktm = nc.dram_tensor("ktm", [ntok, 256], BF16, kind="ExternalInput").ap()
    vtm = nc.dram_tensor("vtm", [ntok, 256], BF16, kind="ExternalInput").ap()
    brow = nc.dram_tensor("brow", [1, ntok], F32, kind="ExternalInput").ap()
    btm = nc.dram_tensor("btm", [128, ntok // 128], F32, kind="ExternalInput").ap()
    msl = nc.dram_tensor("msl", [CC, CC], F32, kind="ExternalInput").ap()
    msu = nc.dram_tensor("msu", [CC, CC], F32, kind="ExternalInput").ap()
    mui = nc.dram_tensor("mui", [CC, CC], F32, kind="ExternalInput").ap()
    onk = nc.dram_tensor("ones_k", [128, 1], BF16, kind="ExternalInput").ap()
    onb = nc.dram_tensor("ones_b", [1, 128], BF16, kind="ExternalInput").ap()
    ob = nc.dram_tensor("ob", [ntok, 256], BF16, kind="ExternalOutput").ap()

    NPB = nt // 128

    with ExitStack() as stk:
        al = Alloc(nc, stk)
        qf_t = al.sb("qf_t", [128, 2 * nt], BF16)
        kf_t = al.sb("kf_t", [128, 2 * nt], BF16)
        ktm_t = al.sb("ktm_t", [128, NPB * 256], BF16)
        vtm_t = al.sb("vtm_t", [128, NPB * 256], BF16)
        sq_t = al.sb("sq_t", [128, 2 * nt], BF16)
        qh_t = al.sb("qh_t", [128, 2 * nt], BF16)
        kh_t = al.sb("kh_t", [128, 2 * nt], BF16)
        kb_t = al.sb("kb_t", [128, 2 * nt], BF16)
        khtm_t = al.sb("khtm_t", [128, NPB * 256], BF16)
        kbtm_t = al.sb("kbtm_t", [128, NPB * 256], BF16)
        vbtm_t = al.sb("vbtm_t", [128, NPB * 256], BF16)
        bt_t = al.sb("bt_t", [128, ntok // 128], F32)
        brow_t = al.sb("brow_t", [1, ntok], F32)
        bf_t = al.sb("bf_t", [1, nt], BF16)
        rq_t = al.sb("rq_t", [1, 2 * nt], F32)
        rqb_t = al.sb("rqb_t", [1, 2 * nt], BF16)
        rtm_t = al.sb("rtm_t", [128, NPB], F32)
        s_f32 = al.sb("s_f32", [128, 2 * 256], F32)
        s_bf = al.sb("s_bf", [128, 2 * 256], BF16)
        pA = al.sb("pA", [64, 6 * CC], F32)
        pG = al.sb("pG", [64, 6 * CC], F32)
        atn_t = al.sb("atn_t", [64, CC], BF16)
        z0 = al.sb("z0", [64, 512], F32)
        z1 = al.sb("z1", [64, 512], F32)
        khc_t = al.sb("khc_t", [64, 256], BF16)
        ut_t = al.sb("ut_t", [64, 256], BF16)
        wf_t = al.sb("wf_t", [128, 2 * CC], F32)
        wfb_t = al.sb("wfb_t", [128, 2 * CC], BF16)
        obuf = al.sb("obuf", [128, 256], BF16)
        msl_t = al.sb("msl_t", [CC, CC], F32)
        msu_t = al.sb("msu_t", [CC, CC], F32)
        mui_t = al.sb("mui_t", [CC, CC], F32)
        onk_t = al.sb("onk_t", [128, 1], BF16)
        onb_t = al.sb("onb_t", [1, 128], BF16)
        ps_big = al.ps("ps_big", [128, 512], F32)
        ps_ag = al.ps("ps_ag", [64, 2 * CC], F32)
        ps_at = al.ps("ps_at", [64, CC], F32)
        ps_p2 = al.ps("ps_p2", [64, 2 * CC], F32)
        ps_z = al.ps("ps_z", [64, 512], F32)
        ps_wf = al.ps("ps_wf", [128, 2 * CC], F32)
        ps_t1 = al.ps("ps_t1", [64, 256], F32)
        ps_o = al.ps("ps_o", [64, 256], F32)
        tk = al.sem("tk")
        block = al.block()

        s = S(nc, block, tk)

        @s.dma("sync")
        def _(eng):
            return [eng.dma_start(out=msl_t, in_=msl[:, :]), eng.dma_start(out=msu_t, in_=msu[:, :]),
                    eng.dma_start(out=mui_t, in_=mui[:, :]), eng.dma_start(out=onk_t, in_=onk[:, :]),
                    eng.dma_start(out=onb_t, in_=onb[:, :]), eng.dma_start(out=bt_t, in_=btm[:, :]),
                    eng.dma_start(out=brow_t, in_=brow[:, :])]

        @s.step("vector")
        def _(eng):
            eng.memset(s_f32, 0.0)
            return eng.memset(s_bf, 0.0)

        qf3 = qf_t.rearrange("p (c t) -> p c t", c=2)
        kf3 = kf_t.rearrange("p (c t) -> p c t", c=2)
        qh3 = qh_t.rearrange("p (c t) -> p c t", c=2)
        kh3 = kh_t.rearrange("p (c t) -> p c t", c=2)
        kb3 = kb_t.rearrange("p (c t) -> p c t", c=2)
        sq3 = sq_t.rearrange("p (c t) -> p c t", c=2)

        def l2norm_fm(src3, src_flat, dst3, roff):
            @s.step("scalar")
            def _(eng):
                return eng.activation(sq_t, src_flat, AF.Square)

            for u in range(nt // 512):
                @s.step("tensor")
                def _(eng, u=u):
                    for c in range(2):
                        i = eng.matmul(ps_big[0:1, 0:512], onk_t, sq3[:, c, u * 512:(u + 1) * 512],
                                       start=(c == 0), stop=(c == 1))
                    return i

                @s.step("vector")
                def _(eng, u=u):
                    return eng.tensor_single_scalar(rq_t[0:1, roff + u * 512:roff + (u + 1) * 512],
                                                    ps_big[0:1, 0:512], 1e-12, ALU.add)

            @s.step("scalar")
            def _(eng):
                return eng.activation(rq_t[0:1, roff:roff + nt], rq_t[0:1, roff:roff + nt], AF.Sqrt)

            @s.step("vector")
            def _(eng):
                eng.reciprocal(rq_t[0:1, roff:roff + nt], rq_t[0:1, roff:roff + nt])
                eng.drain()
                return eng.tensor_copy(rqb_t[0:1, roff:roff + nt], rq_t[0:1, roff:roff + nt])

            for u in range(nt // 512):
                @s.step("tensor")
                def _(eng, u=u):
                    return eng.matmul(ps_big[:, 0:512], onb_t,
                                      rqb_t[0:1, roff + u * 512:roff + (u + 1) * 512],
                                      start=True, stop=True)

                @s.step("vector")
                def _(eng, u=u):
                    am, bm = bcast(src3[:, :, u * 512:(u + 1) * 512],
                                   ps_big[:, 0:512].rearrange("p (o t) -> p o t", o=1))
                    return eng.tensor_tensor(dst3[:, :, u * 512:(u + 1) * 512], am, bm, ALU.mult)

        for j in range(ntile):
            t0 = j * nt

            @s.dma("sync")
            def _(eng, t0=t0):
                return [
                    eng.dma_start(out=qf3, in_=qfm.rearrange("(c p) t -> p c t", p=128)[:, :, t0:t0 + nt]),
                    eng.dma_start(out=kf3, in_=kfm.rearrange("(c p) t -> p c t", p=128)[:, :, t0:t0 + nt]),
                    eng.dma_start(out=ktm_t.rearrange("p (b f) -> p b f", f=256),
                                  in_=ktm.rearrange("(b p) f -> p b f", p=128)[:, t0 // 128:(t0 + nt) // 128, :]),
                    eng.dma_start(out=vtm_t.rearrange("p (b f) -> p b f", f=256),
                                  in_=vtm.rearrange("(b p) f -> p b f", p=128)[:, t0 // 128:(t0 + nt) // 128, :]),
                ]

            l2norm_fm(qf3, qf_t, qh3, 0)
            l2norm_fm(kf3, kf_t, kh3, nt)

            @s.step("vector")
            def _(eng, t0=t0):
                return eng.tensor_copy(bf_t, brow_t[0:1, t0:t0 + nt])

            for u in range(nt // 512):
                @s.step("tensor")
                def _(eng, u=u):
                    return eng.matmul(ps_big[:, 0:512], onb_t, bf_t[0:1, u * 512:(u + 1) * 512],
                                      start=True, stop=True)

                @s.step("vector")
                def _(eng, u=u):
                    am, bm = bcast(kh3[:, :, u * 512:(u + 1) * 512],
                                   ps_big[:, 0:512].rearrange("p (o t) -> p o t", o=1))
                    return eng.tensor_tensor(kb3[:, :, u * 512:(u + 1) * 512], am, bm, ALU.mult)

            ktm3 = ktm_t.rearrange("p (b f) -> p b f", f=256)
            vtm3 = vtm_t.rearrange("p (b f) -> p b f", f=256)
            khtm3 = khtm_t.rearrange("p (b f) -> p b f", f=256)
            kbtm3 = kbtm_t.rearrange("p (b f) -> p b f", f=256)
            vbtm3 = vbtm_t.rearrange("p (b f) -> p b f", f=256)
            sqtm3 = sq_t.rearrange("p (b f) -> p b f", f=256)

            @s.step("scalar")
            def _(eng):
                return eng.activation(sq_t, ktm_t, AF.Square)

            @s.step("vector")
            def _(eng):
                eng.tensor_reduce(rtm_t.rearrange("p (b o) -> p b o", o=1), sqtm3, AXX, ALU.add)
                eng.drain()
                return eng.tensor_single_scalar(rtm_t, rtm_t, 1e-12, ALU.add)

            @s.step("scalar")
            def _(eng):
                return eng.activation(rtm_t, rtm_t, AF.Sqrt)

            @s.step("vector")
            def _(eng, j=j):
                eng.reciprocal(rtm_t, rtm_t)
                eng.drain()
                am, bm = bcast(ktm3, rtm_t.rearrange("p (b o) -> p b o", o=1))
                eng.tensor_tensor(khtm3, am, bm, ALU.mult)
                bts = bt_t[:, j * NPB:(j + 1) * NPB].rearrange("p (b o) -> p b o", o=1)
                am2, bm2 = bcast(khtm3, bts)
                eng.tensor_tensor(kbtm3, am2, bm2, ALU.mult)
                am3, bm3 = bcast(vtm3, bts)
                return eng.tensor_tensor(vbtm3, am3, bm3, ALU.mult)

            for cc in range(nch):
                c0 = cc * CC
                tb, poff = c0 // 128, c0 % 128

                @s.step("vector")
                def _(eng, tb=tb, poff=poff, c0=c0):
                    eng.tensor_copy(z0[:, 0:256], vbtm3[poff:poff + CC, tb, :])
                    eng.tensor_copy(z0[:, 256:512], kbtm3[poff:poff + CC, tb, :])
                    eng.tensor_copy(khc_t, khtm3[poff:poff + CC, tb, :])
                    eng.tensor_copy(wf_t[:, 0:CC], kb3[:, 0, c0:c0 + CC])
                    return eng.tensor_copy(wf_t[:, CC:2 * CC], kb3[:, 1, c0:c0 + CC])

                @s.step("tensor")
                def _(eng, c0=c0):
                    for c in range(2):
                        eng.matmul(ps_ag[:, 0:CC], kb3[:, c, c0:c0 + CC], kh3[:, c, c0:c0 + CC],
                                   start=(c == 0), stop=(c == 1))
                    for c in range(2):
                        eng.matmul(ps_ag[:, CC:2 * CC], kh3[:, c, c0:c0 + CC], kb3[:, c, c0:c0 + CC],
                                   start=(c == 0), stop=(c == 1))
                    for c in range(2):
                        i = eng.matmul(ps_at[:, :], kh3[:, c, c0:c0 + CC], qh3[:, c, c0:c0 + CC],
                                       start=(c == 0), stop=(c == 1))
                    return i

                @s.step("vector")
                def _(eng):
                    eng.tensor_tensor(pA[:, 0:CC], ps_ag[:, 0:CC], msl_t, ALU.mult)
                    eng.tensor_tensor(pG[:, 0:CC], ps_ag[:, CC:2 * CC], msu_t, ALU.mult)
                    return eng.tensor_tensor(atn_t, ps_at[:, :], mui_t, ALU.mult)

                for l in range(1, 6):
                    @s.step("tensor")
                    def _(eng, l=l):
                        eng.matmul(ps_p2[:, 0:CC], pG[:, (l - 1) * CC:l * CC],
                                   pA[:, (l - 1) * CC:l * CC], start=True, stop=True)
                        return eng.matmul(ps_p2[:, CC:2 * CC], pA[:, (l - 1) * CC:l * CC],
                                          pG[:, (l - 1) * CC:l * CC], start=True, stop=True)

                    @s.step("vector")
                    def _(eng, l=l):
                        eng.tensor_copy(pA[:, l * CC:(l + 1) * CC], ps_p2[:, 0:CC])
                        return eng.tensor_copy(pG[:, l * CC:(l + 1) * CC], ps_p2[:, CC:2 * CC])

                zs = [z0, z1]
                for l in range(6):
                    zc, zn = zs[l % 2], zs[(l + 1) % 2]

                    @s.step("tensor")
                    def _(eng, l=l, zc=zc):
                        eng.matmul(ps_wf[:, 0:CC], zc[:, 256:384], pG[:, l * CC:(l + 1) * CC],
                                   start=True, stop=True)
                        eng.matmul(ps_wf[:, CC:2 * CC], zc[:, 384:512], pG[:, l * CC:(l + 1) * CC],
                                   start=True, stop=True)
                        return eng.matmul(ps_z[:, :], pG[:, l * CC:(l + 1) * CC], zc[:, :],
                                          start=True, stop=True)

                    @s.step("vector")
                    def _(eng, l=l, zc=zc, zn=zn):
                        eng.tensor_tensor(wf_t, wf_t, ps_wf[:, :], ALU.add)
                        return eng.tensor_tensor(zn, zc, ps_z[:, :], ALU.add)

                zfin = zs[0]

                @s.step("vector")
                def _(eng):
                    return eng.tensor_copy(wfb_t, wf_t)

                @s.step("tensor")
                def _(eng):
                    for c in range(2):
                        i = eng.matmul(ps_t1[:, :], wfb_t[:, c * CC:(c + 1) * CC],
                                       s_bf[:, c * 256:(c + 1) * 256], start=(c == 0), stop=(c == 1))
                    return i

                @s.step("vector")
                def _(eng, zfin=zfin):
                    return eng.tensor_tensor(ut_t, zfin[:, 0:256], ps_t1[:, :], ALU.subtract)

                @s.step("tensor")
                def _(eng, c0=c0):
                    for c in range(2):
                        eng.matmul(ps_o[:, :], qh3[:, c, c0:c0 + CC],
                                   s_bf[:, c * 256:(c + 1) * 256], start=(c == 0), stop=False)
                    return eng.matmul(ps_o[:, :], atn_t, ut_t, start=False, stop=True)

                @s.step("scalar")
                def _(eng, poff=poff):
                    return eng.copy(obuf[poff:poff + CC, :], ps_o[:, :])

                if poff == 64:
                    @s.dma("sync")
                    def _(eng, t0=t0, tb=tb):
                        return [eng.dma_start(out=ob[t0 + tb * 128:t0 + (tb + 1) * 128, :],
                                              in_=obuf)]

                @s.step("tensor")
                def _(eng):
                    eng.matmul(ps_big[:, 0:256], khc_t[:, 0:128], ut_t, start=True, stop=True)
                    return eng.matmul(ps_big[:, 256:512], khc_t[:, 128:256], ut_t, start=True, stop=True)

                @s.step("vector")
                def _(eng):
                    eng.tensor_tensor(s_f32, s_f32, ps_big[:, :], ALU.add)
                    eng.drain()
                    return eng.tensor_copy(s_bf, s_f32)

        s.fence()
    return nc


def run_delta(prep):
    nc = build_delta()
    cst = const_inputs()
    CC = DLT_C
    msl_np = np.zeros((CC, CC), np.float32)
    for i_ in range(CC):
        msl_np[i_, :i_] = -1.0
    msu_np = np.ascontiguousarray(msl_np.T)
    mui_np = np.triu(np.ones((CC, CC), np.float32))
    in_maps = []
    for c in range(NCORES):
        b, h = divmod(c, 4)
        sl = slice(h * 256, (h + 1) * 256)
        bh = np.asarray(prep["beta"][b][h], np.float32)
        in_maps.append({
            "qfm": np.ascontiguousarray(prep["qf"][b][sl]),
            "kfm": np.ascontiguousarray(prep["kf"][b][sl]),
            "ktm": np.ascontiguousarray(prep["kf"][b][sl].T),
            "vtm": np.ascontiguousarray(prep["vf"][b][sl].T),
            "brow": np.ascontiguousarray(bh.reshape(1, T)),
            "btm": np.ascontiguousarray(bh.reshape(T // 128, 128).T),
            "msl": msl_np, "msu": msu_np, "mui": mui_np, **cst,
        })
    res = run_bass_kernel_spmd(nc, in_maps, core_ids=list(range(NCORES)))
    dfm = []
    for b in range(B):
        dfm.append(np.concatenate(
            [np.ascontiguousarray(np.asarray(res.results[b * 4 + h]["ob"]).T) for h in range(4)], axis=0))
    return dfm


R1K = 1080
R1N = 2176
NCH1 = 17


def build_l1out(ntok=TPC, nt=512):
    nsub = ntok // nt
    nc = bass.Bass("TRN2", target_bir_lowering=False, debug=False)
    xs = nc.dram_tensor("xs", [D, ntok], F32, kind="ExternalInput").ap()
    loc = nc.dram_tensor("loc", [D, ntok], BF16, kind="ExternalInput").ap()
    mid = nc.dram_tensor("mid", [D, ntok], BF16, kind="ExternalInput").ap()
    dlt = nc.dram_tensor("dlt", [D, ntok], BF16, kind="ExternalInput").ap()
    vd = nc.dram_tensor("vd", [D, ntok], BF16, kind="ExternalInput").ap()
    rw1 = nc.dram_tensor("rw1", [R1K, R1N], F32, kind="ExternalInput").ap()
    rb1 = nc.dram_tensor("rb1", [128, NCH1], F32, kind="ExternalInput").ap()
    rw2 = nc.dram_tensor("rw2", [R1N, 16], F32, kind="ExternalInput").ap()
    rb2 = nc.dram_tensor("rb2", [16, 1], F32, kind="ExternalInput").ap()
    mixw = nc.dram_tensor("mixw", [128, DC], F32, kind="ExternalInput").ap()
    onw = nc.dram_tensor("onw", [128, DC], F32, kind="ExternalInput").ap()
    Wo = nc.dram_tensor("Wo", [D, D], F32, kind="ExternalInput").ap()
    id16 = nc.dram_tensor("id16", [16, 16], BF16, kind="ExternalInput").ap()
    id128 = nc.dram_tensor("id128", [128, 128], BF16, kind="ExternalInput").ap()
    esel = nc.dram_tensor("esel", [16, 16 * 128], BF16, kind="ExternalInput").ap()
    onk = nc.dram_tensor("ones_k", [128, 1], BF16, kind="ExternalInput").ap()
    onb = nc.dram_tensor("ones_b", [1, 128], BF16, kind="ExternalInput").ap()
    x5 = nc.dram_tensor("x5", [D, ntok], F32, kind="ExternalOutput").ap()
    fscratch = nc.dram_tensor("fscratch", [8, nt], F32, kind="Internal").ap()

    with ExitStack() as stk:
        al = Alloc(nc, stk)
        xs_t = al.sb("xs_t", [128, DC * nt], F32)
        ln_t = al.sb("ln_t", [128, DC * nt], BF16)
        sq_t = al.sb("sq_t", [128, DC * nt], BF16)
        lntmp_t = al.sb("lntmp_t", [128, DC * nt], F32)
        br_t = [al.sb(f"br{b}_t", [128, DC * nt], BF16) for b in range(4)]
        h1_t = al.sb("h1_t", [128, NCH1 * nt], BF16)
        stage_t = al.sb("stage_t", [1, 8 * nt], F32)
        feats_t = al.sb("feats_t", [64, nt], BF16)
        lgm_t = al.sb("lgm_t", [16, nt], BF16)
        lgtm_t = al.sb("lgtm_t", [128, 4 * 16], F32)
        ptok_t = al.sb("ptok_t", [128, 4 * 16], BF16)
        pfm_t = al.sb("pfm_t", [16, nt], BF16)
        stat_t = al.sb("stat_t", [128, 16], F32)
        on_t = al.sb("on_t", [128, DC * nt], BF16)
        prod_t = on_t
        mixd_t = lntmp_t[:, 0:2 * nt]
        m2w_t = lntmp_t[:, 2 * nt:4 * nt]
        hout = al.sb("hout", [128, nt], F32)
        srow = al.sb("srow", [1, 4 * nt], F32)
        srow_bf = al.sb("srow_bf", [1, 2 * nt], BF16)
        nrow2 = al.sb("nrow2", [1, 2 * nt], F32)
        nrow_bf = al.sb("nrow_bf", [1, nt], BF16)
        w1_b = al.sb("w1_b", [128, DC * R1N], BF16)
        w56_b = al.sb("w56_b", [64, R1N], BF16)
        w2_b = al.sb("w2_b", [128, NCH1 * 16], BF16)
        wo_b = al.sb("wo_b", [128, DC * D], BF16)
        wstage = al.sb("wstage", [128, R1N // 2], F32)
        rb1_t = al.sb("rb1_t", [128, NCH1], F32)
        rb2_t = al.sb("rb2_t", [16, 1], F32)
        mixw_t = al.sb("mixw_t", [128, DC], F32)
        onw_t = al.sb("onw_t", [128, DC], F32)
        id16_t = al.sb("id16_t", [16, 16], BF16)
        id128_t = al.sb("id128_t", [128, 128], BF16)
        esel_t = al.sb("esel_t", [16, 16 * 128], BF16)
        onk_t = al.sb("onk_t", [128, 1], BF16)
        onb_t = al.sb("onb_t", [1, 128], BF16)
        ps_a = al.ps("ps_a", [128, nt], F32)
        ps_b = al.ps("ps_b", [128, nt], F32)
        ps_mm = al.ps("ps_mm", [128, nt], F32)
        ps_row = al.ps("ps_row", [1, nt], F32)
        ps_rowB = al.ps("ps_rowB", [1, nt], F32)
        ps_lg = al.ps("ps_lg", [16, nt], F32)
        ps_t1 = al.ps("ps_t1", [128, 16], BF16)
        ps_t2 = al.ps("ps_t2", [16, 128], BF16)
        tk = al.sem("tk")
        block = al.block()
        ps_stA = ps_a[0:1, :]
        ps_stB = ps_b[0:1, :]

        s = S(nc, block, tk)

        @s.dma("sync")
        def _(eng):
            return [eng.dma_start(out=rb1_t, in_=rb1[:, :]), eng.dma_start(out=rb2_t, in_=rb2[:, :]),
                    eng.dma_start(out=mixw_t, in_=mixw[:, :]), eng.dma_start(out=onw_t, in_=onw[:, :]),
                    eng.dma_start(out=id16_t, in_=id16[:, :]), eng.dma_start(out=id128_t, in_=id128[:, :]),
                    eng.dma_start(out=esel_t, in_=esel[:, :]),
                    eng.dma_start(out=onk_t, in_=onk[:, :]), eng.dma_start(out=onb_t, in_=onb[:, :])]

        for kc in range(DC):
            for hf in range(2):
                @s.dma("sync")
                def _(eng, kc=kc, hf=hf):
                    return [eng.dma_start(out=wstage,
                                          in_=rw1[kc * 128:(kc + 1) * 128, hf * (R1N // 2):(hf + 1) * (R1N // 2)])]

                @s.step("vector")
                def _(eng, kc=kc, hf=hf):
                    return eng.tensor_copy(w1_b[:, kc * R1N + hf * (R1N // 2): kc * R1N + (hf + 1) * (R1N // 2)], wstage)

        for hf in range(2):
            @s.dma("sync")
            def _(eng, hf=hf):
                return [eng.dma_start(out=wstage[0:56, :],
                                      in_=rw1[1024:1080, hf * (R1N // 2):(hf + 1) * (R1N // 2)])]

            @s.step("vector")
            def _(eng, hf=hf):
                return eng.tensor_copy(w56_b[0:56, hf * (R1N // 2):(hf + 1) * (R1N // 2)], wstage[0:56, :])

        @s.dma("sync")
        def _(eng):
            return [eng.dma_start(out=wstage[:, 0:NCH1 * 16].rearrange("p (c f) -> p c f", f=16),
                                  in_=rw2.rearrange("(c p) f -> p c f", p=128))]

        @s.step("vector")
        def _(eng):
            return eng.tensor_copy(w2_b, wstage[:, 0:NCH1 * 16])

        for kc in range(DC):
            @s.dma("sync")
            def _(eng, kc=kc):
                return [eng.dma_start(out=wstage[:, 0:D], in_=Wo[kc * 128:(kc + 1) * 128, :])]

            @s.step("vector")
            def _(eng, kc=kc):
                return eng.tensor_scalar_mul(wo_b[:, kc * D:(kc + 1) * D], wstage[:, 0:D],
                                             onw_t[:, kc:kc + 1])

        xs3 = xs_t.rearrange("p (c t) -> p c t", c=DC)
        ln3 = ln_t.rearrange("p (c t) -> p c t", c=DC)
        sq3 = sq_t.rearrange("p (c t) -> p c t", c=DC)
        lntmp3 = lntmp_t.rearrange("p (c t) -> p c t", c=DC)
        br3 = [b.rearrange("p (c t) -> p c t", c=DC) for b in br_t]
        prod3 = prod_t.rearrange("p (c t) -> p c t", c=DC)
        on3 = on_t.rearrange("p (c t) -> p c t", c=DC)
        S1 = srow[0:1, 0:nt]
        S2 = srow[0:1, nt:2 * nt]
        DEN = srow[0:1, 2 * nt:3 * nt]
        RR = srow[0:1, 3 * nt:4 * nt]

        featsf_t = al.sb("featsf_t", [64, nt], F32)

        for j in range(nsub):
            t0 = j * nt

            @s.dma("sync")
            def _(eng, t0=t0):
                ins = [eng.dma_start(out=xs3, in_=xs.rearrange("(c p) t -> p c t", p=128)[:, :, t0:t0 + nt])]
                for bi, hbm in enumerate((loc, mid, dlt, vd)):
                    ins.append(eng.dma_start(out=br3[bi],
                                             in_=hbm.rearrange("(c p) t -> p c t", p=128)[:, :, t0:t0 + nt]))
                return ins

            emit_ln_fm(s, nc, xs3, ln3, sq3, srow, srow_bf, ps_stA, ps_stB,
                       ps_a, ps_b, lntmp3, onk_t, onb_t, nt)

            SC1 = nrow2[0:1, 0:nt]
            SC2 = nrow2[0:1, nt:2 * nt]

            def stage_slot(r):
                return stage_t[0:1, (r % 8) * nt:(r % 8 + 1) * nt]

            def flush(first_row):
                @s.dma("sync")
                def _(eng, first_row=first_row):
                    return [eng.dma_start(out=fscratch[:, :].rearrange("r t -> (r t)").rearrange("(o f) -> o f", o=1),
                                          in_=stage_t[0:1, :])]

                @s.dma("sync")
                def _(eng, first_row=first_row):
                    return [eng.dma_start(out=featsf_t[first_row:first_row + 8, :], in_=fscratch[:, :])]

            for bi in range(4):
                @s.step("scalar")
                def _(eng, bi=bi):
                    return eng.activation(sq_t, br_t[bi], AF.Square)

                for h in range(4):
                    @s.step("tensor")
                    def _(eng, bi=bi, h=h):
                        eng.matmul(ps_row[0:1, 0:nt], onk_t, br3[bi][:, 2 * h, :], start=True, stop=False)
                        eng.matmul(ps_row[0:1, 0:nt], onk_t, br3[bi][:, 2 * h + 1, :], start=False, stop=True)
                        eng.matmul(ps_rowB[0:1, :], onk_t, sq3[:, 2 * h, :], start=True, stop=False)
                        return eng.matmul(ps_rowB[0:1, :], onk_t, sq3[:, 2 * h + 1, :], start=False, stop=True)

                    @s.step("vector")
                    def _(eng, bi=bi, h=h):
                        mrow = stage_slot(h)
                        vrow = stage_slot(4 + h)
                        eng.tensor_single_scalar(mrow, ps_row[0:1, 0:nt], 1.0 / 256.0, ALU.mult)
                        eng.drain()
                        eng.tensor_single_scalar(SC1, ps_rowB[0:1, :], 1.0 / 255.0, ALU.mult)
                        eng.tensor_tensor(SC2, mrow, mrow, ALU.mult)
                        eng.drain()
                        eng.tensor_single_scalar(SC2, SC2, 256.0 / 255.0, ALU.mult)
                        eng.drain()
                        return eng.tensor_tensor(vrow, SC1, SC2, ALU.subtract)

                flush(bi * 8)

            pairs = [(0, 1), (0, 2), (0, 3), (1, 2), (1, 3), (2, 3)]
            for gi in range(3):
                for pi in range(2):
                    i_, j_ = pairs[gi * 2 + pi]

                    @s.step("vector")
                    def _(eng, i_=i_, j_=j_):
                        return eng.tensor_tensor(prod_t, br_t[i_], br_t[j_], ALU.mult)

                    for h in range(4):
                        @s.step("tensor")
                        def _(eng, h=h):
                            eng.matmul(ps_row[0:1, 0:nt], onk_t, prod3[:, 2 * h, :], start=True, stop=False)
                            return eng.matmul(ps_row[0:1, 0:nt], onk_t, prod3[:, 2 * h + 1, :], start=False, stop=True)

                        @s.step("vector")
                        def _(eng, pi=pi, h=h):
                            return eng.tensor_single_scalar(stage_slot(pi * 4 + h), ps_row[0:1, 0:nt],
                                                            1.0 / 256.0, ALU.mult)

                flush(32 + gi * 8)

            @s.step("vector")
            def _(eng):
                return eng.tensor_copy(feats_t[0:56, :], featsf_t[0:56, :])

            for mc in range(NCH1):
                @s.step("tensor")
                def _(eng, mc=mc):
                    for kc in range(DC):
                        eng.matmul(ps_mm[:, :], w1_b[:, kc * R1N + mc * 128: kc * R1N + (mc + 1) * 128],
                                   ln3[:, kc, :], start=(kc == 0), stop=False)
                    return eng.matmul(ps_mm[:, :], w56_b[0:56, mc * 128:(mc + 1) * 128],
                                      feats_t[0:56, :], start=False, stop=True)

                @s.step("scalar")
                def _(eng, mc=mc):
                    return eng.activation(h1_t[:, mc * nt:(mc + 1) * nt], ps_mm[:, :], AF.Silu,
                                          bias=rb1_t[:, mc:mc + 1])

            @s.step("tensor")
            def _(eng):
                for mc in range(NCH1):
                    i = eng.matmul(ps_lg[:, :], w2_b[:, mc * 16:(mc + 1) * 16],
                                   h1_t[:, mc * nt:(mc + 1) * nt], start=(mc == 0), stop=(mc == NCH1 - 1))
                return i

            @s.step("vector")
            def _(eng):
                return eng.tensor_scalar_add(lgm_t, ps_lg[:, :], rb2_t[:, 0:1])

            for nb in range(nt // 128):
                @s.step("tensor")
                def _(eng, nb=nb):
                    return eng.transpose(ps_t1[:, :], lgm_t[:, nb * 128:(nb + 1) * 128], id16_t)

                @s.step("vector")
                def _(eng, nb=nb):
                    return eng.tensor_copy(lgtm_t[:, nb * 16:(nb + 1) * 16], ps_t1[:, :])

            for nb in range(nt // 128):
                LG = lgtm_t[:, nb * 16:(nb + 1) * 16].rearrange("p (h i) -> p h i", i=4)
                M1 = stat_t[:, 0:4].rearrange("p (h o) -> p h o", o=1)
                SS = stat_t[:, 4:8].rearrange("p (h o) -> p h o", o=1)
                EX = stat_t[:, 8:12] if False else None

                @s.step("vector")
                def _(eng, LG=LG, M1=M1):
                    eng.tensor_reduce(M1, LG, AXX, ALU.max)
                    eng.drain()
                    am, bm = bcast(LG, M1)
                    return eng.tensor_tensor(LG, am, bm, ALU.subtract)

                @s.step("scalar")
                def _(eng, LG=LG):
                    return eng.activation(LG, LG, AF.Exp)

                @s.step("vector")
                def _(eng, LG=LG, SS=SS, nb=nb):
                    eng.tensor_reduce(SS, LG, AXX, ALU.add)
                    eng.drain()
                    eng.reciprocal(stat_t[:, 4:8], stat_t[:, 4:8])
                    eng.drain()
                    am, bm = bcast(LG, SS)
                    eng.tensor_tensor(LG, am, bm, ALU.mult)
                    eng.drain()
                    return eng.tensor_scalar(ptok_t[:, nb * 16:(nb + 1) * 16],
                                             lgtm_t[:, nb * 16:(nb + 1) * 16], 0.96, 0.01,
                                             ALU.mult, ALU.add)

            for nb in range(nt // 128):
                @s.step("tensor")
                def _(eng, nb=nb):
                    return eng.transpose(ps_t2[:, :], ptok_t[:, nb * 16:(nb + 1) * 16], id128_t)

                @s.step("vector")
                def _(eng, nb=nb):
                    return eng.tensor_copy(pfm_t[:, nb * 128:(nb + 1) * 128], ps_t2[:, :])

            for h in range(4):
                for bi in range(4):
                    @s.step("tensor")
                    def _(eng, h=h, bi=bi):
                        r = h * 4 + bi
                        return eng.matmul(ps_a[:, :] if bi % 2 == 0 else ps_b[:, :],
                                          esel_t[:, r * 128:(r + 1) * 128],
                                          pfm_t[:, :], start=True, stop=True)

                    @s.step("vector")
                    def _(eng, h=h, bi=bi):
                        psx = ps_a[:, :] if bi % 2 == 0 else ps_b[:, :]
                        for ci in range(2):
                            mc = 2 * h + ci
                            tgt = mixd_t[:, ci * nt:(ci + 1) * nt]
                            if bi == 0:
                                i = eng.tensor_tensor(tgt, br3[0][:, mc, :], psx, ALU.mult)
                            else:
                                eng.tensor_tensor(m2w_t[:, ci * nt:(ci + 1) * nt],
                                                  br3[bi][:, mc, :], psx, ALU.mult)
                                i = eng.tensor_tensor(tgt, tgt, m2w_t[:, ci * nt:(ci + 1) * nt], ALU.add)
                        return i

                @s.step("scalar")
                def _(eng, h=h):
                    return eng.activation(sq_t[:, 0:2 * nt], mixd_t, AF.Square)

                @s.step("tensor")
                def _(eng):
                    eng.matmul(ps_row[0:1, 0:nt], onk_t, sq_t[:, 0:nt], start=True, stop=False)
                    return eng.matmul(ps_row[0:1, 0:nt], onk_t, sq_t[:, nt:2 * nt], start=False, stop=True)

                @s.step("vector")
                def _(eng, h=h):
                    for ci in range(2):
                        i = eng.tensor_scalar_mul(m2w_t[:, ci * nt:(ci + 1) * nt],
                                                  mixd_t[:, ci * nt:(ci + 1) * nt],
                                                  mixw_t[:, 2 * h + ci:2 * h + ci + 1])
                    return i

                @s.step("scalar")
                def _(eng):
                    return eng.activation(sq_t[:, 0:2 * nt], m2w_t, AF.Square)

                @s.step("tensor")
                def _(eng):
                    eng.matmul(ps_rowB[0:1, :], onk_t, sq_t[:, 0:nt], start=True, stop=False)
                    return eng.matmul(ps_rowB[0:1, :], onk_t, sq_t[:, nt:2 * nt], start=False, stop=True)

                @s.step("vector")
                def _(eng):
                    eng.tensor_single_scalar(S1, ps_row[0:1, 0:nt], 1.0 / 256.0, ALU.mult)
                    eng.tensor_single_scalar(S2, ps_rowB[0:1, :], 1.0 / 256.0, ALU.mult)
                    eng.drain()
                    eng.tensor_single_scalar(S1, S1, LN_EPS, ALU.add)
                    eng.drain()
                    eng.tensor_single_scalar(S1, S1, LN_EPS, ALU.mult)
                    eng.drain()
                    return eng.tensor_tensor(DEN, S2, S1, ALU.add)

                @s.step("scalar")
                def _(eng):
                    return eng.activation(DEN, DEN, AF.Sqrt)

                @s.step("vector")
                def _(eng):
                    eng.reciprocal(RR, DEN)
                    eng.drain()
                    return eng.tensor_copy(nrow_bf[0:1, :], RR)

                @s.step("tensor")
                def _(eng):
                    return eng.matmul(ps_a[:, :], onb_t, nrow_bf[0:1, :], start=True, stop=True)

                @s.step("vector")
                def _(eng, h=h):
                    for ci in range(2):
                        am, bm = bcast(m2w_t[:, ci * nt:(ci + 1) * nt],
                                       ps_a[:, :].rearrange("p (o t) -> p o t", o=1)[:, 0, :]
                                       if False else ps_a[:, :])
                        i = eng.tensor_tensor(on3[:, 2 * h + ci, :], m2w_t[:, ci * nt:(ci + 1) * nt],
                                              ps_a[:, :], ALU.mult)
                    return i

            for mc in range(DC):
                @s.step("tensor")
                def _(eng, mc=mc):
                    for kc in range(DC):
                        i = eng.matmul(ps_mm[:, :], wo_b[:, kc * D + mc * 128: kc * D + (mc + 1) * 128],
                                       on3[:, kc, :], start=(kc == 0), stop=(kc == DC - 1))
                    return i

                @s.step("vector")
                def _(eng, mc=mc):
                    return eng.tensor_tensor(hout, xs3[:, mc, :], ps_mm[:, :], ALU.add)

                @s.dma("sync")
                def _(eng, mc=mc, j=j):
                    return [eng.dma_start(out=x5[mc * 128:(mc + 1) * 128, j * nt:(j + 1) * nt],
                                          in_=hout)]

        s.fence()
    return nc


def l1out_inputs(i):
    rw1 = np.zeros((R1K, R1N), np.float32)
    rw1[:, 0:2160] = np.asarray(i["L1_attn_r_w1"], np.float32)
    rw2 = np.zeros((R1N, 16), np.float32)
    rw2[0:2160, :] = np.asarray(i["L1_attn_r_w2"], np.float32)
    rb1 = np.zeros((R1N,), np.float32)
    rb1[0:2160] = np.asarray(i["L1_attn_r_b1"], np.float32)
    esel = np.zeros((16, 16 * 128), dtype=BF)
    for r in range(16):
        esel[r, r * 128:(r + 1) * 128] = 1.0
    return {
        "rw1": rw1, "rw2": rw2,
        "rb1": np.ascontiguousarray(rb1.reshape(NCH1, 128).T),
        "rb2": np.ascontiguousarray(np.asarray(i["L1_attn_r_b2"], np.float32).reshape(16, 1)),
        "mixw": vec_tile(np.asarray(i["L1_attn_mix_w"], np.float32).reshape(-1)),
        "onw": vec_tile(np.tile(np.asarray(i["L1_attn_onorm_w"], np.float32), 4)),
        "Wo": np.asarray(i["L1_attn_Wo"], np.float32),
        "id16": np.eye(16, dtype=BF), "id128": np.eye(128, dtype=BF), "esel": esel,
    }


def run_l1out(x4_fm, prep, dfm, oinputs):
    nc = build_l1out()
    cst = const_inputs()
    in_maps = []
    for c in range(NCORES):
        b, q = divmod(c, 4)
        sl = slice(q * TPC, (q + 1) * TPC)
        in_maps.append({
            "xs": np.ascontiguousarray(x4_fm[b][:, sl]),
            "loc": np.ascontiguousarray(prep["loc"][b][:, sl]),
            "mid": np.ascontiguousarray(prep["mid"][b][:, sl]),
            "dlt": np.ascontiguousarray(dfm[b][:, sl]),
            "vd": np.ascontiguousarray(prep["vf"][b][:, sl]),
            **oinputs, **cst,
        })
    res = run_bass_kernel_spmd(nc, in_maps, core_ids=list(range(NCORES)))
    return [np.asarray(r["x5"], np.float32) for r in res.results]
